# version 58
# baseline (speedup 1.0000x reference)
import numpy as np
import ml_dtypes
from contextlib import ExitStack
import concourse.bass as bass
import concourse.mybir as mybir
from concourse.bass_utils import run_bass_kernel_spmd
from concourse.bass_core_rust import IndirectOffsetOnAxis

F32 = mybir.dt.float32
BF16 = mybir.dt.bfloat16
AX = mybir.AxisListType.X
ALU = mybir.AluOpType
AF = mybir.ActivationFunctionType

D = 1024
HB = 8
TC = 2048
TH = 2304
NCH = 18
NT = 16
NE = 8
CAP = 512
NCHK = NE * CAP // 128
NEG = -1.0e30


class Cx:

    def __init__(self, me, E, sems):
        self.me = me
        self.E = E
        self.sems = sems
        self.c = {k: 0 for k in sems}

    def on(self, who):
        return self.me == who

    def w(self, who, val):
        if val > 0:
            self.E.wait_ge(self.sems[who], val)

    def bump(self, who, ins=None, n=1):
        self.c[who] += n
        if ins is not None:
            ins.then_inc(self.sems[who], n)
        return self.c[who]


def build_program(debug=False):
    nc = bass.Bass("TRN2", target_bir_lowering=False)

    ap = lambda *a, **k: nc.dram_tensor(*a, kind="ExternalInput", **k).ap()
    x_halo = ap("x_halo", [TH, D], F32)
    wq = ap("wq", [D, D], F32)
    wk = ap("wk", [D, 128], F32)
    wv = ap("wv", [D, 128], F32)
    wo = ap("wo", [D, D], F32)
    gate_t = ap("gate_t", [D, NE], F32)
    expert_t = ap("expert_t", [NE, D, D], F32)
    bqc = ap("bqc", [128, 8], F32)
    bkr = ap("bkr", [1, 128], F32)
    bvr = ap("bvr", [1, 128], F32)
    bor = ap("bor", [1, D], F32)
    ident = ap("ident", [128, 128], F32)
    ident_bf = ap("ident_bf", [128, 128], BF16)
    mask_a = ap("mask_a", [128, 384], F32)
    mask_b = ap("mask_b", [128, 384], F32)
    edge = ap("edge", [1, TH], F32)
    e_pat = ap("e_pat", [128, 128], F32)
    xn_d = nc.dram_tensor("xn_d", [TC, D], BF16, kind="Internal").ap()
    xp_d = nc.dram_tensor("xp_d", [NE * CAP, D], BF16, kind="Internal").ap()
    zp_d = nc.dram_tensor("zp_d", [NE * CAP, D], BF16, kind="Internal").ap()
    y = nc.dram_tensor("y", [TC, D], F32, kind="ExternalOutput").ap()
    dbg = {}
    if debug:
        for nm, shape in [("d_xnT", [128, HB * TH]), ("d_KT", [64, TH]),
                          ("d_V", [128, NCH * 128]), ("d_masks", [128, NT * NE]),
                          ("d_g", [128, NT * D]), ("d_QT", [64, 2048]),
                          ("d_OT", [128, D]), ("d_den", [128, 8]),
                          ("d_nmx", [128, 8]), ("d_ET", [128, 2 * 4 * 384])]:
            dbg[nm] = nc.dram_tensor(nm, shape, F32, kind="ExternalOutput").ap()

    es = ExitStack()
    sb = lambda n, s, dt=F32: es.enter_context(nc.sbuf_tensor(n, s, dt))
    psa = lambda n, s, dt=F32: es.enter_context(nc.psum_tensor(n, s, dt))
    sem = lambda n: es.enter_context(nc.semaphore(n))

    with es:
        xnT = sb("xnT", [128, HB * TH], BF16)
        KT0 = sb("KT0", [64, TH], BF16)
        KT1 = sb("KT1", [64, TH], BF16)
        V = sb("V", [128, NCH * 128], BF16)
        g_acc = sb("g_acc", [128, NT * D], BF16)
        w_q = sb("w_q", [128, HB * D], BF16)
        w_o = sb("w_o", [128, HB * D], BF16)
        w_k = sb("w_k", [128, HB * 128], BF16)
        w_v = sb("w_v", [128, HB * 128], BF16)
        gate_bf = sb("gate_bf", [128, HB * NE], BF16)
        bq_cols = sb("bq_cols", [128, 8], F32)
        bk_row = sb("bk_row", [1, 128], BF16)
        bv_row = sb("bv_row", [1, 128], BF16)
        bo_row = sb("bo_row", [1, D], BF16)
        id_bf = sb("id_bf", [128, 128], BF16)
        id_32 = sb("id_32", [128, 128], F32)
        ones_32 = sb("ones_32", [1, 128], F32)
        mka = sb("mka", [128, 384], BF16)
        mkb = sb("mkb", [128, 384], BF16)
        edge_bf = sb("edge_bf", [1, TH], BF16)
        ones_row = sb("ones_row", [1, 128], BF16)
        zero_row = sb("zero_row", [1, 512], BF16)
        masks = sb("masks", [128, NT * NE], F32)
        eps_col = sb("eps_col", [128, 1], F32)
        stats6 = sb("stats6", [128, 24], F32)
        stats2 = sb("stats2", [128, 4], F32)
        std_c = sb("std_c", [128, 3], F32)
        r_c = sb("r_c", [128, 2], F32)
        VT_scr = sb("VT_scr", [128, 256], BF16)
        bscr = sb("bscr", [128, 6144], BF16)
        E_scr = bscr[:, 0:1536]
        E2 = bscr[:, 1536:3072]
        xr = bscr[:, 0:2048]
        xnTp = bscr[:, 2048:4096]
        zev = bscr[:, 4096:6144]
        e_pat_sb = sb("e_pat_sb", [128, 128], F32)
        gidx_i = sb("gidx_i", [128, 16], mybir.dt.int32)
        ET = bscr[:, 3072:4608]
        OT = sb("OT", [128, D], BF16)
        nmx = sb("nmx", [128, 2 * 4], F32)
        den = sb("den", [128, 2 * 4], F32)
        lnden = sb("lnden", [128, 2 * 4], F32)
        rden = sb("rden", [128, 2 * 4], F32)
        bias2 = sb("bias2", [128, 2 * 4], F32)
        arena = sb("arena", [128, 6144], F32)
        x_ch = [arena[:, 0:1024], arena[:, 1024:2048],
                arena[:, 2048:3072], arena[:, 3072:4096]]
        x_ch2 = [arena[:, 0:1024], arena[:, 1024:2048]]
        sg_scr = [arena[:, 2048:3072], arena[:, 3072:4096]]
        out_st = [arena[:, 4096:5120], arena[:, 5120:6144]]
        maskT = arena[0:8, 0:2048]
        Rr = arena[0:8, 2048:4096]
        Rcol = arena[:, 4096:4224]
        Rtok = arena[:, 4224:4240]
        etok = arena[:, 4240:4256]
        gidx_f = arena[:, 4256:4272]
        emsk = arena[:, 4272:4400]
        bfar = sb("bfar", [128, 16384], BF16)
        w_e = [bfar[:, 0:8192], bfar[:, 8192:16384]]
        QT = [[bfar[0:64, 0:4096], bfar[0:64, 4096:8192]],
              [bfar[0:64, 8192:12288], bfar[0:64, 12288:16384]]]
        xnbf = [bfar[:, 8192:9216], bfar[:, 9216:10240]]
        KTg = [KT0, KT1]

        psA = psa("psA", [128, 2048], F32)
        psB = psa("psB", [128, 1024], F32)
        psT = psa("psT", [128, 2048], BF16)
        psC = psB[:, 0:512]

        sems = {
            "t": sem("sT"), "v": sem("sV"), "s": sem("sS"),
            "q": sem("sQ"), "qa": sem("sQa"), "qb": sem("sQb"),
            "qg": sem("sQg"), "qh": sem("sQh"),
            "g": sem("sG"), "g2": sem("sG2"),
            "ge0": sem("sGe0"), "ge1": sem("sGe1"),
            "oa": sem("sOa"), "ob": sem("sOb"),
            "qc": sem("sQc"), "qd": sem("sQd"),
            "qe": sem("sQe"), "qf": sem("sQf"),
            "sa": sem("sSa"), "sb": sem("sSb"), "sc": sem("sSc"), "sd": sem("sSd"),
        }

        def script(me, E):
            cx = Cx(me, E, sems)
            ev = {}

            ins = None
            if cx.on("sync"):
                ins = E.dma_start(out=id_bf[:], in_=ident_bf)
            cx.bump("q", ins, 16)
            ins = None
            if cx.on("gpsimd"):
                ins = E.dma_start(out=mka[:], in_=mask_a)
            cx.bump("g", ins, 16)
            ins = None
            if cx.on("gpsimd"):
                ins = E.dma_start(out=mkb[:], in_=mask_b)
            cx.bump("g", ins, 16)
            ins = None
            if cx.on("gpsimd"):
                ins = E.dma_start(out=bk_row[:], in_=bkr)
            cx.bump("g", ins, 16)
            ins = None
            if cx.on("gpsimd"):
                ins = E.dma_start(out=bv_row[:], in_=bvr)
            cx.bump("g", ins, 16)
            ins = None
            if cx.on("gpsimd"):
                ins = E.dma_start(out=bo_row[:], in_=bor)
            cx.bump("g", ins, 16)
            ins = None
            if cx.on("gpsimd"):
                ins = E.dma_start(out=edge_bf[:], in_=edge)
            cx.bump("g", ins, 16)
            ins = None
            if cx.on("gpsimd"):
                ins = E.dma_start(out=gate_bf[:].rearrange("p (b e) -> p b e", b=HB),
                                  in_=gate_t.rearrange("(b p) e -> p b e", p=128))
            cx.bump("g", ins, 16)
            ins = None
            if cx.on("gpsimd"):
                ins = E.dma_start(out=w_k[:].rearrange("p (b d) -> p b d", b=HB),
                                  in_=wk.rearrange("(b p) d -> p b d", p=128))
            cx.bump("g", ins, 16)
            ins = None
            if cx.on("gpsimd"):
                ins = E.dma_start(out=w_v[:].rearrange("p (b d) -> p b d", b=HB),
                                  in_=wv.rearrange("(b p) d -> p b d", p=128))
            cx.bump("g", ins, 16)
            ev["setup_g_small"] = cx.c["g"]
            ins = None
            if cx.on("gpsimd"):
                ins = E.dma_start(out=w_e[0][:, :].rearrange("p (b d) -> p b d", b=HB),
                                  in_=expert_t[0].rearrange("(b p) d -> p b d", p=128))
            cx.bump("ge0", ins, 16)
            ev["we", 0] = ("ge0", cx.c["ge0"])
            ins = None
            if cx.on("gpsimd"):
                ins = E.dma_start(out=w_q[:].rearrange("p (b d) -> p b d", b=HB),
                                  in_=wq.rearrange("(b p) d -> p b d", p=128))
            cx.bump("g2", ins, 16)
            ins = None
            if cx.on("gpsimd"):
                ins = E.dma_start(out=w_o[:].rearrange("p (b d) -> p b d", b=HB),
                                  in_=wo.rearrange("(b p) d -> p b d", p=128))
            cx.bump("g2", ins, 16)
            ev["wqo"] = cx.c["g2"]

            ins = None
            if cx.on("sync"):
                ins = E.dma_start(out=bq_cols[:], in_=bqc)
            cx.bump("q", ins, 16)
            ins = None
            if cx.on("sync"):
                ins = E.dma_start(out=id_32[:], in_=ident)
            cx.bump("q", ins, 16)
            ins = None
            if cx.on("sync"):
                ins = E.dma_start(out=e_pat_sb[:], in_=e_pat)
            cx.bump("q", ins, 16)
            ev["setup_q"] = cx.c["q"]
            ins = None
            if cx.on("vector"):
                ins = E.memset(xr[:, 0:1024], 0.0)
            cx.bump("v", ins)
            ev["xr0_zero"] = cx.c["v"]
            ins = None
            if cx.on("gpsimd"):
                cx.w("v", ev["xr0_zero"])
                ins = E.dma_start(out=xp_d.rearrange("(a p) d -> p a d", p=128),
                                  in_=xr[:, 0:1024].unsqueeze(1).broadcast_to([128, NCHK, D]))
            cx.bump("sa", ins, 16)
            ev["xp_zero"] = ("sa", cx.c["sa"])

            ins = None
            if cx.on("vector"):
                cx.w("g", ev["setup_g_small"])
                E.memset(ones_row[:], 1.0)
                E.memset(zero_row[:], 0.0)
                E.memset(ones_32[:], 1.0)
                ins = E.memset(eps_col[:], 1e-5)
            cx.bump("v", ins)
            ev["setup_v"] = cx.c["v"]

            ev["xch_free", -4] = 0
            ev["xch_free", -3] = 0
            ev["xch_free", -2] = 0
            ev["xch_free", -1] = 0
            std_c3 = std_c

            def p0_ln_a(ch):
                c4 = ch % 4
                ins = None
                if cx.on("sync"):
                    cx.w("v", ev["xch_free", ch - 4])
                    ins = E.dma_start(out=x_ch[c4], in_=x_halo[ch * 128:(ch + 1) * 128, :])
                qs_ = ["qa", "qb", "qg", "qh"][c4]
                cx.bump(qs_, ins, 16)
                ev["xdma", ch] = (qs_, cx.c[qs_])

            def p0_ln_b(ch):
                p = ch % 2
                ins = None
                if cx.on("vector"):
                    cx.w(ev["xdma", ch][0], ev["xdma", ch][1])
                    if ch >= 2:
                        cx.w("s", ev["sqrt", ch - 2])
                    E.bn_stats(stats6[:, p * 12:p * 12 + 6], x_ch[ch % 4][:, 0:512])
                    ins = E.bn_stats(stats6[:, p * 12 + 6:p * 12 + 12], x_ch[ch % 4][:, 512:1024])
                cx.bump("v", ins)
                ins = None
                if cx.on("vector"):
                    cx.w("v", cx.c["v"])
                    ins = E.bn_aggr(stats2[:, p * 2:(p + 1) * 2], stats6[:, p * 12:p * 12 + 12])
                cx.bump("v", ins)
                ev["stats", ch] = cx.c["v"]

            def p0_ln_c(ch):
                p = ch % 2
                ins = None
                if cx.on("scalar"):
                    cx.w("v", ev["stats", ch])
                    if ch >= 2:
                        cx.w("v", ev["xn", ch - 2])
                    ins = E.activation(out=std_c[:, p:p + 1], in_=stats2[:, p * 2 + 1:p * 2 + 2],
                                       func=AF.Sqrt, bias=eps_col[:], scale=1.0)
                cx.bump("s", ins)
                ev["sqrt", ch] = cx.c["s"]

            def p0_ln_d(ch):
                p = ch % 2
                ins = None
                if cx.on("vector"):
                    cx.w("s", ev["sqrt", ch])
                    if ch >= 3 and ("xnw", ch - 2) in ev:
                        cx.w(*ev["xnw", ch - 2])
                    ins = E.reciprocal(r_c[:, p:p + 1], std_c[:, p:p + 1])
                cx.bump("v", ins)
                ins = None
                if cx.on("vector"):
                    cx.w("v", cx.c["v"])
                    ins = E.tensor_scalar(out=xnbf[p][:, :], in0=x_ch[ch % 4],
                                          scalar1=stats2[:, p * 2:p * 2 + 1],
                                          scalar2=r_c[:, p:p + 1],
                                          op0=ALU.subtract, op1=ALU.mult)
                cx.bump("v", ins)
                ev["xn", ch] = cx.c["v"]
                ev["xch_free", ch] = cx.c["v"]

                if 1 <= ch <= 16:
                    qs2 = "qc" if ch % 2 == 1 else "qd"
                    ins = None
                    if cx.on("sync"):
                        cx.w("v", ev["xn", ch])
                        ins = E.dma_start(out=xn_d[(ch - 1) * 128:ch * 128, :],
                                          in_=xnbf[ch % 2][:, :])
                    cx.bump(qs2, ins, 16)
                    ev["xnw", ch] = (qs2, cx.c[qs2])

            def p0_ln_e(ch):
                p = ch % 2
                ins = None
                if cx.on("tensor"):
                    cx.w("v", ev["xn", ch])
                    cx.w("q", ev["setup_q"])
                    if ch >= 2:
                        cx.w("s", ev["xnT", ch - 2])
                    for hb in range(HB):
                        ins = E.transpose(psT[:, p * 1024 + hb * 128:p * 1024 + (hb + 1) * 128],
                                          xnbf[p][:, hb * 128:(hb + 1) * 128], id_bf[:])
                cx.bump("t", ins)
                ev["xtr", ch] = cx.c["t"]

                ins = None
                if cx.on("scalar"):
                    cx.w("t", ev["xtr", ch])
                    dst = xnT[:].rearrange("p (b c) -> p b c", b=HB)[:, 0:HB, ch * 128:(ch + 1) * 128]
                    src = psT[:, p * 1024:p * 1024 + 1024].rearrange("p (b c) -> p b c", b=HB)
                    ins = E.copy(out=dst, in_=src)
                cx.bump("s", ins)
                ev["xnT", ch] = cx.c["s"]

            def p0_gate(ch):
                greg = psB[:, (ch % 2) * 512:(ch % 2) * 512 + 8]
                if 1 <= ch <= 16:
                    ins = None
                    if cx.on("tensor"):
                        cx.w("s", ev["xnT", ch])
                        if ch >= 3:
                            cx.w("v", ev["mask", ch - 2])
                        for hb in range(HB):
                            ins = E.matmul(greg,
                                           xnT[:, hb * TH + ch * 128: hb * TH + (ch + 1) * 128],
                                           gate_bf[:, hb * NE:(hb + 1) * NE],
                                           start=(hb == 0), stop=(hb == HB - 1))
                    cx.bump("t", ins)
                    ev["gate", ch] = cx.c["t"]
                else:
                    ev["gate", ch] = cx.c["t"]

            def p0_proj(ch):
                p = ch % 2
                kreg = psA[:, p * 512:p * 512 + 128]
                vreg = psA[:, 1024 + p * 512:1024 + p * 512 + 128]
                greg = psB[:, p * 512:p * 512 + 8]
                ins = None
                if cx.on("tensor"):
                    cx.w("s", ev["xnT", ch])
                    if ch >= 2:
                        cx.w("s", ev["kev", ch - 2])
                    for hb in range(HB):
                        E.matmul(kreg, w_k[:, hb * 128:(hb + 1) * 128],
                                 xnT[:, hb * TH + ch * 128: hb * TH + (ch + 1) * 128],
                                 start=(hb == 0), stop=False)
                    ins = E.matmul(kreg, bk_row[:],
                                   edge_bf[0:1, ch * 128:(ch + 1) * 128],
                                   start=False, stop=True)
                cx.bump("t", ins)
                ev["kmm", ch] = cx.c["t"]

                ins = None
                if cx.on("tensor"):
                    if ch >= 2:
                        cx.w("s", ev["vev2", ch - 2])
                    for hb in range(HB):
                        E.matmul(vreg,
                                 xnT[:, hb * TH + ch * 128: hb * TH + (ch + 1) * 128],
                                 w_v[:, hb * 128:(hb + 1) * 128],
                                 start=(hb == 0), stop=False)
                    ins = E.matmul(vreg, edge_bf[0:1, ch * 128:(ch + 1) * 128],
                                   bv_row[:], start=False, stop=True)
                cx.bump("t", ins)
                ev["vmm", ch] = cx.c["t"]

                ins = None
                if cx.on("scalar"):
                    cx.w("t", ev["kmm", ch])
                    E.copy(out=KT0[0:64, ch * 128:(ch + 1) * 128], in_=kreg[0:64, :])
                    ins = E.copy(out=KT1[0:64, ch * 128:(ch + 1) * 128], in_=kreg[64:128, :])
                cx.bump("s", ins)
                ev["kev", ch] = cx.c["s"]

                ins = None
                if cx.on("scalar"):
                    cx.w("t", ev["vmm", ch])
                    ins = E.copy(out=V[:, ch * 128:(ch + 1) * 128], in_=vreg)
                cx.bump("s", ins)
                ev["vev2", ch] = cx.c["s"]

                if 1 <= ch <= 16:
                    t_idx = ch - 1
                    ins = None
                    if cx.on("vector"):
                        cx.w("t", ev["gate", ch])
                        ins = E.reduce_max(std_c[:, 2:3], greg, axis=AX)
                    cx.bump("v", ins)
                    ins = None
                    if cx.on("vector"):
                        cx.w("v", cx.c["v"])
                        ins = E.tensor_scalar(out=masks[:, t_idx * NE:(t_idx + 1) * NE],
                                              in0=greg, scalar1=std_c[:, 2:3],
                                              scalar2=None, op0=ALU.is_ge)
                    cx.bump("v", ins)
                    ev["mask", ch] = cx.c["v"]
                else:
                    ev["mask", ch] = cx.c["v"]

            for ch in range(NCH + 1):
                if ch < NCH:
                    p0_ln_a(ch)
                    p0_ln_b(ch)
                    p0_ln_c(ch)
                if 0 <= ch - 2 < NCH:
                    p0_gate(ch - 2)
                if 0 <= ch - 1 < NCH:
                    p0_ln_d(ch - 1)
                    p0_ln_e(ch - 1)
                if 0 <= ch - 2 < NCH:
                    p0_proj(ch - 2)
            p0_proj(NCH - 1)

            ev["p0_t"] = cx.c["t"]
            ev["p0_s"] = cx.c["s"]
            ev["p0_v"] = cx.c["v"]

            NA = NT * 8

            def coords(n):
                return (n // 8, (n // 4) % 2, n % 4)

            def emit_prologue(G):
                for mb in range(HB):
                    qreg = psB[:, (mb % 2) * 512:(mb % 2) * 512 + 512]
                    ins = None
                    if cx.on("tensor"):
                        if G == 0 and mb == 0:
                            cx.w("g2", ev["wqo"])
                            cx.w("s", ev["g", NT - 1])
                            cx.w("s", ev["zev", NCHK - 1])
                        if mb >= 2:
                            cx.w("v", ev["qt_ev", G, mb - 2])
                        elif G >= 1:
                            cx.w("v", ev["qt_ev", G - 1, HB - 2 + mb])
                        if G >= 1 and mb <= 1:
                            cx.w("v", ev["combm", G * 4 - 1])
                        for kb in range(HB):
                            ins = E.matmul(qreg,
                                           w_q[:, kb * D + mb * 128: kb * D + (mb + 1) * 128],
                                           xnT[:, kb * TH + 128 + G * 512: kb * TH + 128 + (G + 1) * 512],
                                           start=(kb == 0), stop=(kb == HB - 1))
                    cx.bump("t", ins)
                    ev["qt_mm", G, mb] = cx.c["t"]

                    ins = None
                    if cx.on("vector"):
                        cx.w("t", ev["qt_mm", G, mb])
                        if G == 0 and mb == 0:
                            cx.w("q", ev["setup_q"])
                        E.tensor_scalar(out=QT[0][G % 2][0:64, mb * 512:(mb + 1) * 512],
                                        in0=qreg[0:64, :],
                                        scalar1=bq_cols[0:64, mb:mb + 1],
                                        scalar2=None, op0=ALU.add)
                        ins = E.tensor_scalar(out=QT[1][G % 2][0:64, mb * 512:(mb + 1) * 512],
                                              in0=qreg[64:128, :],
                                              scalar1=bq_cols[64:128, mb:mb + 1],
                                              scalar2=None, op0=ALU.add)
                    cx.bump("v", ins)
                    ev["qt_ev", G, mb] = cx.c["v"]

            def emit_passA(n):
                t, g, hq = coords(n)
                G, tl = t // 4, t % 4
                h_ = n % 2
                base = h_ * 1024
                heads = [g * 8 + hq * 2 + i for i in range(2)]
                ins = None
                if cx.on("tensor"):
                    if n % 32 < 8:
                        cx.w("v", ev["qt_ev", G, hq * 2 + 1])
                    if n >= 2:
                        p2 = coords(n - 2)
                        cx.w("s", ev[("pAexp",) + p2])
                        cx.w("v", ev[("mx",) + p2])
                    for i in range(2):
                        E.matmul(psA[:, base + i * 512:base + i * 512 + 384],
                                 id_bf[:], mka[:], start=True, stop=False)
                    for i, H in enumerate(heads):
                        j = hq * 2 + i
                        qs = QT[g][G % 2][:,
                                          j * 512 + tl * 128:j * 512 + (tl + 1) * 128]
                        ins = E.matmul(psA[:, base + i * 512:base + i * 512 + 384], qs,
                                       KTg[g][:, t * 128: t * 128 + 384],
                                       start=False, stop=True)
                cx.bump("t", ins)
                ev["pA", t, g, hq] = cx.c["t"]

            def e_mx(n):
                t, g, hq = coords(n)
                p = n % 2
                base = (n % 2) * 1024
                ins = None
                if cx.on("vector"):
                    cx.w("t", ev["pA", t, g, hq])
                    if n >= 2:
                        cx.w("s", ev[("pAexp",) + coords(n - 2)])
                    ps3 = psA[:, base:base + 1024].rearrange(
                        "p (a b) -> p a b", a=2)[:, 0:2, 0:384]
                    ins = E.reduce_max(nmx[:, p * 2:p * 2 + 2], ps3,
                                       axis=AX, negate=True)
                cx.bump("v", ins)
                ev["mx", t, g, hq] = cx.c["v"]

            def e_exp(n):
                t, g, hq = coords(n)
                p = n % 2
                h_ = n % 2
                base = h_ * 1024
                ins = None
                if cx.on("scalar"):
                    cx.w("v", ev["mx", t, g, hq])
                    if n >= 2:
                        cx.w("v", ev[("norm",) + coords(n - 2)])
                    for i in range(2):
                        ins = E.activation(out=E_scr[:, (h_ * 2 + i) * 384:(h_ * 2 + i + 1) * 384],
                                           in_=psA[:, base + i * 512:base + i * 512 + 384],
                                           func=AF.Exp, bias=nmx[:, p * 2 + i:p * 2 + i + 1],
                                           scale=1.0, accum_out=den[:, p * 2 + i:p * 2 + i + 1])
                cx.bump("s", ins)
                ev["pAexp", t, g, hq] = cx.c["s"]

            def e_norm(n):
                t, g, hq = coords(n)
                p = n % 2
                h_ = n % 2
                ins = None
                if cx.on("vector"):
                    cx.w("s", ev["pAexp", t, g, hq])
                    if n >= 2:
                        cx.w("t", ev[("ptr",) + coords(n - 2)])
                    ins = E.reciprocal(rden[:, p * 2:p * 2 + 2], den[:, p * 2:p * 2 + 2])
                cx.bump("v", ins)
                ins = None
                if cx.on("vector"):
                    cx.w("v", cx.c["v"])
                    for i in range(2):
                        ins = E.tensor_scalar(out=E2[:, (p * 2 + i) * 384:(p * 2 + i + 1) * 384],
                                              in0=E_scr[:, (h_ * 2 + i) * 384:(h_ * 2 + i + 1) * 384],
                                              scalar1=rden[:, p * 2 + i:p * 2 + i + 1],
                                              scalar2=None, op0=ALU.mult)
                cx.bump("v", ins)
                ev["norm", t, g, hq] = cx.c["v"]

            def e_tr(n):
                t, g, hq = coords(n)
                p = n % 2
                ins = None
                if cx.on("tensor"):
                    cx.w("v", ev["norm", t, g, hq])
                    if n >= 2:
                        cx.w("v", ev[("etev",) + coords(n - 2)])
                    for i in range(2):
                        for kb in range(3):
                            ins = E.transpose(
                                psT[:, p * 1024 + i * 384 + kb * 128: p * 1024 + i * 384 + (kb + 1) * 128],
                                E2[:, (p * 2 + i) * 384 + kb * 128:(p * 2 + i) * 384 + (kb + 1) * 128],
                                id_bf[:])
                cx.bump("t", ins)
                ev["ptr", t, g, hq] = cx.c["t"]

            def e_etev(n):
                t, g, hq = coords(n)
                p = n % 2
                prev = coords(n - 1) if n >= 1 else None
                ins = None
                if cx.on("vector"):
                    cx.w("t", ev["ptr", t, g, hq])
                    if n == 0:
                        cx.w(*ev["zpw", NCHK - 1])
                    if prev is not None:
                        cx.w("t", ev[("ot",) + prev])
                    ins = E.tensor_copy(out=ET[:, p * 768:(p + 1) * 768],
                                        in_=psT[:, p * 1024:p * 1024 + 768])
                cx.bump("v", ins)
                ev["etev", t, g, hq] = cx.c["v"]

            def e_ot(n):
                t, g, hq = coords(n)
                p = n % 2
                heads = [g * 8 + hq * 2 + i for i in range(2)]
                ins = None
                if cx.on("tensor"):
                    cx.w("v", ev["etev", t, g, hq])
                    if (g, hq) == (0, 0):
                        if t % 4 == 0:
                            cx.w("v", ev["qt_ev", t // 4, HB - 1])
                        if t >= 1:
                            cx.w("v", ev["combmA", t - 1])
                        E.matmul(psB[:, 0:512], ones_row[:], zero_row[:],
                                 start=True, stop=False, skip_group_check=True)
                        if t >= 1:
                            cx.w("v", ev["combm", t - 1])
                        E.matmul(psB[:, 512:1024], ones_row[:], zero_row[:],
                                 start=True, stop=False, skip_group_check=True)
                    for kb in range(3):
                        vs = V[:, (t + kb) * 128 + g * 64:(t + kb) * 128 + g * 64 + 64]
                        for i, H in enumerate(heads):
                            j = hq * 2 + i
                            ins = E.matmul(psB[64 * g:64 * g + 64,
                                               j * 128:j * 128 + 128],
                                           vs,
                                           ET[:, p * 768 + i * 384 + kb * 128: p * 768 + i * 384 + (kb + 1) * 128],
                                           start=False,
                                           stop=(kb == 2 and i == 1 and hq % 2 == 1),
                                           skip_group_check=True)
                cx.bump("t", ins)
                ev["ot", t, g, hq] = cx.c["t"]

            comb_sched = {}

            def emit_comb_mult(t):
                ins = None
                if cx.on("vector"):
                    cx.w("t", ev["sattn", t])
                    ins = E.tensor_tensor(out=sg_scr[t % 2][:, 0:512], in0=psB[:, 0:512],
                                          in1=g_acc[:, t * D:t * D + 512], op=ALU.mult)
                cx.bump("v", ins)
                ev["combmA", t] = cx.c["v"]
                ins = None
                if cx.on("vector"):
                    ins = E.tensor_tensor(out=sg_scr[t % 2][:, 512:1024], in0=psB[:, 512:1024],
                                          in1=g_acc[:, t * D + 512:(t + 1) * D], op=ALU.mult)
                cx.bump("v", ins)
                ev["combm", t] = cx.c["v"]

            def emit_comb_add(t):
                ins = None
                if cx.on("vector"):
                    cx.w("v", ev["combm", t])
                    cx.w(*ev["xdma2", t])
                    if t >= 2:
                        os_sem = "oa" if t % 2 == 0 else "ob"
                        cx.w(os_sem, 16 * (t // 2))
                    ins = E.tensor_tensor(out=out_st[t % 2], in0=sg_scr[t % 2],
                                          in1=x_ch2[t % 2], op=ALU.add)
                cx.bump("v", ins)
                ev["comb", t] = cx.c["v"]
                ins = None
                if cx.on("scalar"):
                    cx.w("v", ev["comb", t])
                    ins = E.dma_start(out=y[t * 128:(t + 1) * 128, :], in_=out_st[t % 2])
                cx.bump("oa" if t % 2 == 0 else "ob", ins, 16)

            def emit_tile_bottom(t, defer=True):
                ins = None
                if cx.on("scalar"):
                    cx.w("t", ev["ot", t, 1, 3])
                    ins = E.copy(out=OT[:, 0:256], in_=psB[:, 0:256])
                cx.bump("s", ins)
                ev["otev0", t] = cx.c["s"]
                ins = None
                if cx.on("scalar"):
                    ins = E.copy(out=OT[:, 256:512], in_=psB[:, 256:512])
                cx.bump("s", ins)
                ev["otev1", t] = cx.c["s"]
                ins = None
                if cx.on("scalar"):
                    ins = E.copy(out=OT[:, 512:1024], in_=psB[:, 512:1024])
                cx.bump("s", ins)
                ev["otev", t] = cx.c["s"]

                ins = None
                if cx.on("tensor"):
                    cx.w("s", ev["otev0", t])
                    cx.w("g2", ev["wqo"])
                    E.matmul(psB[:, 0:512], ones_row[:], bo_row[0:1, 0:512],
                             start=True, stop=False)
                    for dqb in range(2):
                        os_ = OT[:, dqb * 128:(dqb + 1) * 128]
                        E.matmul(psB[:, 0:512], os_, w_o[:, dqb * D: dqb * D + 512],
                                 start=False, stop=False)
                    cx.w("s", ev["otev1", t])
                    for dqb in range(2, 4):
                        os_ = OT[:, dqb * 128:(dqb + 1) * 128]
                        E.matmul(psB[:, 0:512], os_, w_o[:, dqb * D: dqb * D + 512],
                                 start=False, stop=False)
                    cx.w("s", ev["otev", t])
                    E.matmul(psB[:, 512:1024], ones_row[:], bo_row[0:1, 512:1024],
                             start=True, stop=False)
                    for dqb in range(4):
                        os_ = OT[:, dqb * 128:(dqb + 1) * 128]
                        E.matmul(psB[:, 512:1024], os_, w_o[:, dqb * D + 512:(dqb + 1) * D],
                                 start=False, stop=False)
                    for dqb in range(4, HB):
                        os_ = OT[:, dqb * 128:(dqb + 1) * 128]
                        E.matmul(psB[:, 0:512], os_, w_o[:, dqb * D: dqb * D + 512],
                                 start=False, stop=(dqb == HB - 1))
                        ins = E.matmul(psB[:, 512:1024], os_, w_o[:, dqb * D + 512:(dqb + 1) * D],
                                       start=False, stop=(dqb == HB - 1))
                cx.bump("t", ins)
                ev["sattn", t] = cx.c["t"]

                for td in ([0, 1] if t == 0 else ([t + 1] if t < NT - 1 else [])):
                    qs_ = "qa" if td % 2 == 0 else "qb"
                    ins = None
                    if cx.on("sync"):
                        if td >= 2:
                            cx.w("v", ev["comb", td - 2])
                        else:
                            cx.w("v", ev["xch_free", 16 + (td % 2)])
                        ins = E.dma_start(out=x_ch2[td % 2],
                                          in_=x_halo[128 + td * 128: 256 + td * 128, :])
                    cx.bump(qs_, ins, 16)
                    ev["xdma2", td] = (qs_, cx.c[qs_])

                if defer:
                    nb = 8 * t + 10
                    comb_sched[nb + 1] = ("m", t)
                    comb_sched[nb + 2] = ("a", t)
                else:
                    emit_comb_mult(t)
                    emit_comb_add(t)

            ins = None
            if cx.on("tensor"):
                cx.w("s", ev["vev2", NCH - 1])
                cx.w("v", ev["mask", 16])
                cx.w("q", ev["setup_q"])
                for t in range(NT):
                    ins = E.transpose(psA[0:8, t * 128:(t + 1) * 128],
                                      masks[:, t * NE:(t + 1) * NE], id_32[:])
            cx.bump("t", ins)
            ev["pk_mt"] = cx.c["t"]

            ins = None
            if cx.on("scalar"):
                cx.w("t", ev["pk_mt"])
                ins = E.copy(out=maskT, in_=psA[0:8, 0:2048])
            cx.bump("s", ins)
            ev["pk_mtev"] = cx.c["s"]

            ins = None
            if cx.on("vector"):
                cx.w("s", ev["pk_mtev"])
                E.tensor_tensor_scan(out=Rr, data0=maskT, data1=maskT,
                                     initial=0.0, op0=ALU.add, op1=ALU.max)
                E.drain()
                ins = E.tensor_tensor(out=Rr, in0=Rr, in1=maskT, op=ALU.mult)
            cx.bump("v", ins)
            ev["pk_scan"] = cx.c["v"]

            ins = None
            if cx.on("tensor"):
                cx.w("v", ev["pk_scan"])
                for t in range(NT):
                    ins = E.transpose(psB[:, t * 8:(t + 1) * 8],
                                      Rr[0:8, t * 128:(t + 1) * 128], id_32[0:8, 0:8])
            cx.bump("t", ins)
            ev["pk_rt"] = cx.c["t"]

            ins = None
            if cx.on("scalar"):
                cx.w("t", ev["pk_rt"])
                ins = E.copy(out=Rcol, in_=psB[:, 0:128])
            cx.bump("s", ins)
            ev["pk_rcev"] = cx.c["s"]

            ins = None
            if cx.on("vector"):
                cx.w("s", ev["pk_rcev"])
                E.reduce_sum(Rtok, Rcol.rearrange("p (t e) -> p t e", e=NE), axis=AX)
                E.tensor_tensor(out=emsk, in0=masks[:], in1=e_pat_sb[:], op=ALU.mult)
                E.drain()
                E.reduce_sum(etok, emsk.rearrange("p (t e) -> p t e", e=NE), axis=AX)
                E.drain()
                E.tensor_scalar(out=gidx_f, in0=Rtok, scalar1=-1.0, scalar2=float(CAP - 1),
                                op0=ALU.add, op1=ALU.min)
                E.drain()
                E.scalar_tensor_tensor(out=gidx_f, in0=etok, scalar=float(CAP),
                                       in1=gidx_f, op0=ALU.mult, op1=ALU.add)
                E.drain()
                E.tensor_scalar_min(out=gidx_f, in0=gidx_f, scalar1=float(NE * CAP - 1))
                E.drain()
                ins = E.tensor_copy(out=gidx_i[:], in_=gidx_f)
            cx.bump("v", ins)
            ev["pk_gidx"] = cx.c["v"]

            for b2 in range(8):
                buf = xr if b2 % 2 == 0 else xnTp
                qs2 = "qc" if b2 % 2 == 0 else "qd"
                ins = None
                if cx.on("sync"):
                    if b2 == 0:
                        cx.w(*ev["xp_zero"])
                        for chh in range(1, 17):
                            cx.w(*ev["xnw", chh])
                    if b2 >= 2:
                        cx.w(*ev["pk_schb", b2 - 2])
                    ins = E.dma_start(
                        out=buf[:, 0:2048].rearrange("p (a d) -> p a d", a=2),
                        in_=xn_d[b2 * 256:(b2 + 1) * 256, :].rearrange("(a p) d -> p a d", p=128))
                cx.bump(qs2, ins, 16)
                ev["pk_rdhb", b2] = (qs2, cx.c[qs2])

                ss = "sa" if b2 % 2 == 0 else "sb"
                for q in range(2):
                    t = b2 * 2 + q
                    ins = None
                    if cx.on("gpsimd"):
                        if b2 == 0 and q == 0:
                            cx.w("v", ev["pk_gidx"])
                            cx.w(*ev["xp_zero"])
                        if q == 0:
                            cx.w(*ev["pk_rdhb", b2])
                        ins = E.indirect_dma_start(
                            out=xp_d, out_offset=IndirectOffsetOnAxis(gidx_i[:, t:t + 1], 0),
                            in_=buf[:, q * 1024:(q + 1) * 1024], in_offset=None)
                    cx.bump(ss, ins, 16)
                ev["pk_schb", b2] = (ss, cx.c[ss])

            def ffn_front(cc):
                e = cc // 4
                pcc = cc % 2
                e = cc // 4
                pcc = cc % 2
                if cc % 4 == 1 and e + 1 < NE:
                    en = e + 1
                    ins = None
                    if cx.on("gpsimd"):
                        if en == 1:
                            cx.w("t", ev["p0_t"])
                        else:
                            cx.w("t", ev["ffn_z", (en - 2) * 4 + 3])
                        ins = E.dma_start(out=w_e[en % 2][:, :].rearrange("p (b d) -> p b d", b=HB),
                                          in_=expert_t[en].rearrange("(b p) d -> p b d", p=128))
                    ges = "ge0" if en % 2 == 0 else "ge1"
                    cx.bump(ges, ins, 16)
                    ev["we", en] = (ges, cx.c[ges])
                qs2 = "qe" if pcc == 0 else "qf"
                ins = None
                if cx.on("sync"):
                    if cc <= 1:
                        cx.w(*ev["pk_schb", 6])
                        cx.w(*ev["pk_schb", 7])
                    if cc >= 2:
                        cx.w("t", ev["ffn_tr", cc - 2])
                    ins = E.dma_start(out=xr[:, pcc * 1024:(pcc + 1) * 1024],
                                      in_=xp_d[cc * 128:(cc + 1) * 128, :])
                cx.bump(qs2, ins, 16)
                ev["xpr", cc] = (qs2, cx.c[qs2])

                ins = None
                if cx.on("tensor"):
                    cx.w(*ev["xpr", cc])
                    if cc >= 2:
                        cx.w("s", ev["ffn_xev", cc - 2])
                    for hb in range(HB):
                        ins = E.transpose(psT[:, pcc * 1024 + hb * 128:pcc * 1024 + (hb + 1) * 128],
                                          xr[:, pcc * 1024 + hb * 128:pcc * 1024 + (hb + 1) * 128],
                                          id_bf[:])
                cx.bump("t", ins)
                ev["ffn_tr", cc] = cx.c["t"]

                ins = None
                if cx.on("scalar"):
                    cx.w("t", ev["ffn_tr", cc])
                    if cc >= 2:
                        cx.w("t", ev["ffn_z", cc - 2])
                    ins = E.copy(out=xnTp[:, pcc * 1024:(pcc + 1) * 1024],
                                 in_=psT[:, pcc * 1024:(pcc + 1) * 1024])
                cx.bump("s", ins)
                ev["ffn_xev", cc] = cx.c["s"]


            def ffn_back(cc):
                e = cc // 4
                pcc = cc % 2
                zb = psA[:, 0:1024] if pcc == 0 else psA[:, 1024:2048]
                ins = None
                if cx.on("tensor"):
                    cx.w("s", ev["ffn_xev", cc])
                    if cc == 0:
                        cx.w(*ev["we", 0])
                    if e > 0 and cc % 4 == 0:
                        cx.w(*ev["we", e])
                    if cc >= 2:
                        cx.w("s", ev["zev", cc - 2])
                    for hb in range(HB):
                        stat = xnTp[:, pcc * 1024 + hb * 128:pcc * 1024 + (hb + 1) * 128]
                        E.matmul(zb[:, 0:512], stat,
                                 w_e[e % 2][:, hb * D: hb * D + 512],
                                 start=(hb == 0), stop=(hb == HB - 1))
                        ins = E.matmul(zb[:, 512:1024], stat,
                                       w_e[e % 2][:, hb * D + 512: (hb + 1) * D],
                                       start=(hb == 0), stop=(hb == HB - 1))
                cx.bump("t", ins)
                ev["ffn_z", cc] = cx.c["t"]

                ins = None
                if cx.on("scalar"):
                    cx.w("t", ev["ffn_z", cc])
                    if cc >= 2:
                        cx.w(*ev["zpw", cc - 2])
                    ins = E.copy(out=zev[:, pcc * 1024:(pcc + 1) * 1024], in_=zb)
                cx.bump("s", ins)
                ev["zev", cc] = cx.c["s"]

                qs3 = "sa" if pcc == 0 else "sb"
                ins = None
                if cx.on("gpsimd"):
                    cx.w("s", ev["zev", cc])
                    ins = E.dma_start(out=zp_d[cc * 128:(cc + 1) * 128, :],
                                      in_=zev[:, pcc * 1024:(pcc + 1) * 1024])
                cx.bump(qs3, ins, 16)
                ev["zpw", cc] = (qs3, cx.c[qs3])


            for cc in range(NCHK + 1):
                if cc < NCHK:
                    ffn_front(cc)
                if cc >= 1:
                    ffn_back(cc - 1)

            for t in range(NT):
                gq = ["sa", "sb", "sc", "sd"][t % 4]
                ins = None
                if cx.on("gpsimd"):
                    if t <= 3:
                        for cc in range(NCHK):
                            cx.w(*ev["zpw", cc])
                    else:
                        cx.w("v", ev["g_add", t - 4])
                    ins = E.indirect_dma_start(
                        out=g_acc[:, t * D:(t + 1) * D], out_offset=None,
                        in_=zp_d, in_offset=IndirectOffsetOnAxis(gidx_i[:, t:t + 1], 0))
                cx.bump(gq, ins, 16)
                ev["gsel", t] = (gq, cx.c[gq])

                pg = t % 2
                ins = None
                if cx.on("tensor"):
                    if t <= 1:
                        cx.w("s", ev["ffn_xev", NCHK - 1])
                    else:
                        cx.w("v", ev["g_add", t - 2])
                    for hb in range(HB):
                        ins = E.transpose(psT[:, pg * 1024 + hb * 128:pg * 1024 + (hb + 1) * 128],
                                          xnT[:, hb * TH + 128 + t * 128: hb * TH + 256 + t * 128],
                                          id_bf[:])
                cx.bump("t", ins)
                ev["g_tr", t] = cx.c["t"]

                ins = None
                if cx.on("vector"):
                    cx.w("t", ev["g_tr", t])
                    cx.w(*ev["gsel", t])
                    if t >= 2:
                        cx.w("s", ev["g", t - 2])
                    ins = E.tensor_tensor(out=sg_scr[t % 2], in0=psT[:, pg * 1024:pg * 1024 + 1024],
                                          in1=g_acc[:, t * D:(t + 1) * D], op=ALU.add)
                cx.bump("v", ins)
                ev["g_add", t] = cx.c["v"]

                ins = None
                if cx.on("scalar"):
                    cx.w("v", ev["g_add", t])
                    ins = E.activation(out=g_acc[:, t * D:(t + 1) * D],
                                       in_=sg_scr[t % 2], func=AF.Silu)
                cx.bump("s", ins)
                ev["g", t] = cx.c["s"]

            for G in range(4):
                n0 = 32 * G
                emit_prologue(G)
                for n in range(n0, n0 + 32):
                    emit_passA(n)
                    if n - 1 >= n0:
                        e_mx(n - 1)
                        e_exp(n - 1)
                    if n - 2 >= n0:
                        e_norm(n - 2)
                    if n in comb_sched:
                        kind, tc = comb_sched[n]
                        (emit_comb_mult if kind == "m" else emit_comb_add)(tc)
                    if n - 2 >= n0:
                        e_tr(n - 2)
                    if n - 3 >= n0:
                        e_etev(n - 3)
                        e_ot(n - 3)
                        if (n - 3) % 8 == 7:
                            emit_tile_bottom((n - 3) // 8)
                ne = n0 + 32
                e_mx(ne - 1); e_exp(ne - 1)
                e_norm(ne - 2); e_tr(ne - 2)
                e_etev(ne - 3); e_ot(ne - 3)
                e_norm(ne - 1); e_tr(ne - 1)
                e_etev(ne - 2); e_ot(ne - 2)
                e_etev(ne - 1); e_ot(ne - 1)
                emit_tile_bottom(4 * G + 3, defer=False)

            n_dbg = 0
            if debug:
                taps = [("d_xnT", xnT[:]), ("d_KT", KT0[0:64, :]), ("d_V", V[:]),
                        ("d_masks", masks[:]), ("d_g", g_acc[:]),
                        ("d_QT", QT[0][0][0:64, 0:2048]), ("d_OT", OT[:]),
                        ("d_den", den[:]), ("d_nmx", nmx[:]), ("d_ET", ET[:])]
                n_dbg = len(taps)
                if cx.on("gpsimd"):
                    cx.w("t", cx.c["t"])
                    cx.w("v", cx.c["v"])
                    cx.w("s", cx.c["s"])
                    for nm, src_ap in taps:
                        E.dma_start(out=dbg[nm], in_=src_ap).then_inc(sems["g"], 16)
                cx.c["g"] += 16 * n_dbg

            if cx.on("sync"):
                cx.w("oa", 16 * (NT // 2))
                cx.w("ob", 16 * (NT // 2))
                if n_dbg:
                    cx.w("g", ev["setup_g_small"] + 16 * n_dbg)

        with nc.Block() as block:
            @block.sync
            def _(E):
                script("sync", E)

            @block.gpsimd
            def _(E):
                script("gpsimd", E)

            @block.tensor
            def _(E):
                script("tensor", E)

            @block.vector
            def _(E):
                script("vector", E)

            @block.scalar
            def _(E):
                script("scalar", E)

    return nc


def _host_inputs(L0_attn_x, L0_attn_Wq, L0_attn_bq, L0_attn_Wk, L0_attn_bk,
                 L0_attn_Wv, L0_attn_bv, L0_attn_Wo, L0_attn_bo,
                 L0_ffn_gate_w, L0_ffn_expert_w):
    f = np.float32
    x = np.asarray(L0_attn_x, f)
    B, S, _ = x.shape
    perm = np.concatenate([np.concatenate([np.arange(64 * j, 64 * j + 64),
                                           np.arange(512 + 64 * j, 512 + 64 * j + 64)])
                           for j in range(8)])
    shared = {
        "wq": np.ascontiguousarray(np.asarray(L0_attn_Wq, f)[:, perm]),
        "wk": np.ascontiguousarray(np.asarray(L0_attn_Wk, f)),
        "wv": np.ascontiguousarray(np.asarray(L0_attn_Wv, f)),
        "wo": np.ascontiguousarray(np.asarray(L0_attn_Wo, f)[perm, :]),
        "gate_t": np.ascontiguousarray(np.asarray(L0_ffn_gate_w, f).T),
        "expert_t": np.ascontiguousarray(np.asarray(L0_ffn_expert_w, f).transpose(0, 2, 1)),
        "bqc": np.ascontiguousarray(np.asarray(L0_attn_bq, f)[perm].reshape(8, 128).T),
        "bkr": np.ascontiguousarray(np.asarray(L0_attn_bk, f)[None, :]),
        "bvr": np.ascontiguousarray(np.asarray(L0_attn_bv, f)[None, :]),
        "bor": np.ascontiguousarray(np.asarray(L0_attn_bo, f)[None, :]),
        "ident": np.eye(128, dtype=f),
        "ident_bf": np.eye(128, dtype=ml_dtypes.bfloat16),
    }
    i = np.arange(128)
    j = np.arange(128)
    M0 = np.where(j[None, :] >= i[:, None], 0.0, NEG).astype(f)
    M2 = np.where(j[None, :] < i[:, None], 0.0, NEG).astype(f)
    Z = np.zeros((128, 128), f)
    shared["mask_a"] = np.ascontiguousarray(np.concatenate([M0, Z, M2], axis=1))
    shared["mask_b"] = np.ascontiguousarray(
        np.concatenate([M0.T.copy(), Z, M2.T.copy()], axis=1))
    shared["e_pat"] = np.ascontiguousarray(
        np.tile(np.arange(8, dtype=f)[None, :], (128, 16)).reshape(128, 128))

    in_maps = []
    for c in range(8):
        b, q = c // 4, c % 4
        lo = q * TC - 128
        hi = q * TC + TC + 128
        xh = np.zeros((TH, D), f)
        eg = np.zeros((1, TH), f)
        lo2, hi2 = max(lo, 0), min(hi, S)
        xh[lo2 - lo: hi2 - lo] = x[b, lo2:hi2]
        eg[0, lo2 - lo: hi2 - lo] = 1.0
        m = dict(shared)
        m["x_halo"] = xh
        m["edge"] = eg
        in_maps.append(m)
    return in_maps


def kernel(**inputs):
    in_maps = _host_inputs(**inputs)
    nc = build_program()
    res = run_bass_kernel_spmd(nc, in_maps, core_ids=list(range(8)))
    out = np.empty((2, 8192, D), np.float32)
    for c in range(8):
        b, q = c // 4, c % 4
        out[b, q * TC:(q + 1) * TC] = res.results[c]["y"]
    return out


# version 59
# speedup vs baseline: 1.0298x; 1.0298x over previous
import numpy as np
import ml_dtypes
from contextlib import ExitStack
import concourse.bass as bass
import concourse.mybir as mybir
from concourse.bass_utils import run_bass_kernel_spmd
from concourse.bass_core_rust import IndirectOffsetOnAxis

F32 = mybir.dt.float32
BF16 = mybir.dt.bfloat16
AX = mybir.AxisListType.X
ALU = mybir.AluOpType
AF = mybir.ActivationFunctionType

D = 1024
HB = 8
TC = 2048
TH = 2304
NCH = 18
NT = 16
NE = 8
CAP = 512
NCHK = NE * CAP // 128
NEG = -1.0e30


class Cx:

    def __init__(self, me, E, sems):
        self.me = me
        self.E = E
        self.sems = sems
        self.c = {k: 0 for k in sems}

    def on(self, who):
        return self.me == who

    def w(self, who, val):
        if val > 0:
            self.E.wait_ge(self.sems[who], val)

    def bump(self, who, ins=None, n=1):
        self.c[who] += n
        if ins is not None:
            ins.then_inc(self.sems[who], n)
        return self.c[who]


def build_program(debug=False):
    nc = bass.Bass("TRN2", target_bir_lowering=False)

    ap = lambda *a, **k: nc.dram_tensor(*a, kind="ExternalInput", **k).ap()
    x_halo = ap("x_halo", [TH, D], F32)
    wq = ap("wq", [D, D], F32)
    wk = ap("wk", [D, 128], F32)
    wv = ap("wv", [D, 128], F32)
    wo = ap("wo", [D, D], F32)
    gate_t = ap("gate_t", [D, NE], F32)
    expert_t = ap("expert_t", [NE, D, D], F32)
    bqc = ap("bqc", [128, 8], F32)
    bkr = ap("bkr", [1, 128], F32)
    bvr = ap("bvr", [1, 128], F32)
    bor = ap("bor", [1, D], F32)
    ident = ap("ident", [128, 128], F32)
    ident_bf = ap("ident_bf", [128, 128], BF16)
    mask_a = ap("mask_a", [128, 384], F32)
    mask_b = ap("mask_b", [128, 384], F32)
    edge = ap("edge", [1, TH], F32)
    e_pat = ap("e_pat", [128, 128], F32)
    xn_d = nc.dram_tensor("xn_d", [TC, D], BF16, kind="Internal").ap()
    xp_d = nc.dram_tensor("xp_d", [NE * CAP, D], BF16, kind="Internal").ap()
    zp_d = nc.dram_tensor("zp_d", [NE * CAP, D], BF16, kind="Internal").ap()
    y = nc.dram_tensor("y", [TC, D], F32, kind="ExternalOutput").ap()
    dbg = {}
    if debug:
        for nm, shape in [("d_xnT", [128, HB * TH]), ("d_KT", [64, TH]),
                          ("d_V", [128, NCH * 128]), ("d_masks", [128, NT * NE]),
                          ("d_g", [128, NT * D]), ("d_QT", [64, 2048]),
                          ("d_OT", [128, D]), ("d_den", [128, 8]),
                          ("d_nmx", [128, 8]), ("d_ET", [128, 2 * 4 * 384])]:
            dbg[nm] = nc.dram_tensor(nm, shape, F32, kind="ExternalOutput").ap()

    es = ExitStack()
    sb = lambda n, s, dt=F32: es.enter_context(nc.sbuf_tensor(n, s, dt))
    psa = lambda n, s, dt=F32: es.enter_context(nc.psum_tensor(n, s, dt))
    sem = lambda n: es.enter_context(nc.semaphore(n))

    with es:
        xnT = sb("xnT", [128, HB * TH], BF16)
        KT0 = sb("KT0", [64, TH], BF16)
        KT1 = sb("KT1", [64, TH], BF16)
        V = sb("V", [128, NCH * 128], BF16)
        g_acc = sb("g_acc", [128, NT * D], BF16)
        w_q = sb("w_q", [128, HB * D], BF16)
        w_o = sb("w_o", [128, HB * D], BF16)
        w_k = sb("w_k", [128, HB * 128], BF16)
        w_v = sb("w_v", [128, HB * 128], BF16)
        gate_bf = sb("gate_bf", [128, HB * NE], BF16)
        bq_cols = sb("bq_cols", [128, 8], F32)
        bk_row = sb("bk_row", [1, 128], BF16)
        bv_row = sb("bv_row", [1, 128], BF16)
        bo_row = sb("bo_row", [1, D], BF16)
        id_bf = sb("id_bf", [128, 128], BF16)
        id_32 = sb("id_32", [128, 128], F32)
        ones_32 = sb("ones_32", [1, 128], F32)
        mka = sb("mka", [128, 384], BF16)
        mkb = sb("mkb", [128, 384], BF16)
        edge_bf = sb("edge_bf", [1, TH], BF16)
        ones_row = sb("ones_row", [1, 128], BF16)
        zero_row = sb("zero_row", [1, 512], BF16)
        masks = sb("masks", [128, NT * NE], F32)
        eps_col = sb("eps_col", [128, 1], F32)
        stats6 = sb("stats6", [128, 24], F32)
        stats2 = sb("stats2", [128, 4], F32)
        std_c = sb("std_c", [128, 3], F32)
        r_c = sb("r_c", [128, 2], F32)
        VT_scr = sb("VT_scr", [128, 256], BF16)
        bscr = sb("bscr", [128, 6144], BF16)
        E_scr = bscr[:, 0:1536]
        E2 = bscr[:, 1536:3072]
        xr = bscr[:, 0:2048]
        xnTp = bscr[:, 2048:4096]
        zev = bscr[:, 4096:6144]
        e_pat_sb = sb("e_pat_sb", [128, 128], F32)
        gidx_i = sb("gidx_i", [128, 16], mybir.dt.int32)
        ET = bscr[:, 3072:4608]
        OT = sb("OT", [128, D], BF16)
        nmx = sb("nmx", [128, 2 * 4], F32)
        den = sb("den", [128, 2 * 4], F32)
        lnden = sb("lnden", [128, 2 * 4], F32)
        rden = sb("rden", [128, 2 * 4], F32)
        bias2 = sb("bias2", [128, 2 * 4], F32)
        arena = sb("arena", [128, 6144], F32)
        x_ch = [arena[:, 0:1024], arena[:, 1024:2048],
                arena[:, 2048:3072], arena[:, 3072:4096]]
        x_ch2 = [arena[:, 0:1024], arena[:, 1024:2048]]
        sg_scr = [arena[:, 2048:3072], arena[:, 3072:4096]]
        out_st = [arena[:, 4096:5120], arena[:, 5120:6144]]
        maskT = arena[0:8, 0:2048]
        Rr = arena[0:8, 2048:4096]
        Rcol = arena[:, 4096:4224]
        Rtok = arena[:, 4224:4240]
        etok = arena[:, 4240:4256]
        gidx_f = arena[:, 4256:4272]
        emsk = arena[:, 4272:4400]
        bfar = sb("bfar", [128, 16384], BF16)
        w_e = [bfar[:, 0:8192], bfar[:, 8192:16384]]
        QT = [[bfar[0:64, 0:4096], bfar[0:64, 4096:8192]],
              [bfar[0:64, 8192:12288], bfar[0:64, 12288:16384]]]
        xnbf = [bfar[:, 8192:9216], bfar[:, 9216:10240]]
        KTg = [KT0, KT1]

        psA = psa("psA", [128, 2048], F32)
        psB = psa("psB", [128, 1024], F32)
        psT = psa("psT", [128, 2048], BF16)
        psC = psB[:, 0:512]

        sems = {
            "t": sem("sT"), "v": sem("sV"), "s": sem("sS"),
            "q": sem("sQ"), "qa": sem("sQa"), "qb": sem("sQb"),
            "qg": sem("sQg"), "qh": sem("sQh"),
            "g": sem("sG"), "g2": sem("sG2"),
            "ge0": sem("sGe0"), "ge1": sem("sGe1"),
            "oa": sem("sOa"), "ob": sem("sOb"),
            "qc": sem("sQc"), "qd": sem("sQd"),
            "qe": sem("sQe"), "qf": sem("sQf"),
            "sa": sem("sSa"), "sb": sem("sSb"), "sc": sem("sSc"), "sd": sem("sSd"),
        }

        def script(me, E):
            cx = Cx(me, E, sems)
            ev = {}

            ins = None
            if cx.on("sync"):
                ins = E.dma_start(out=id_bf[:], in_=ident_bf)
            cx.bump("q", ins, 16)
            ins = None
            if cx.on("gpsimd"):
                ins = E.dma_start(out=mka[:], in_=mask_a)
            cx.bump("g", ins, 16)
            ins = None
            if cx.on("gpsimd"):
                ins = E.dma_start(out=mkb[:], in_=mask_b)
            cx.bump("g", ins, 16)
            ins = None
            if cx.on("gpsimd"):
                ins = E.dma_start(out=bk_row[:], in_=bkr)
            cx.bump("g", ins, 16)
            ins = None
            if cx.on("gpsimd"):
                ins = E.dma_start(out=bv_row[:], in_=bvr)
            cx.bump("g", ins, 16)
            ins = None
            if cx.on("gpsimd"):
                ins = E.dma_start(out=bo_row[:], in_=bor)
            cx.bump("g", ins, 16)
            ins = None
            if cx.on("gpsimd"):
                ins = E.dma_start(out=edge_bf[:], in_=edge)
            cx.bump("g", ins, 16)
            ins = None
            if cx.on("gpsimd"):
                ins = E.dma_start(out=gate_bf[:].rearrange("p (b e) -> p b e", b=HB),
                                  in_=gate_t.rearrange("(b p) e -> p b e", p=128))
            cx.bump("g", ins, 16)
            ins = None
            if cx.on("gpsimd"):
                ins = E.dma_start(out=w_k[:].rearrange("p (b d) -> p b d", b=HB),
                                  in_=wk.rearrange("(b p) d -> p b d", p=128))
            cx.bump("g", ins, 16)
            ins = None
            if cx.on("gpsimd"):
                ins = E.dma_start(out=w_v[:].rearrange("p (b d) -> p b d", b=HB),
                                  in_=wv.rearrange("(b p) d -> p b d", p=128))
            cx.bump("g", ins, 16)
            ev["setup_g_small"] = cx.c["g"]

            ins = None
            if cx.on("sync"):
                ins = E.dma_start(out=bq_cols[:], in_=bqc)
            cx.bump("q", ins, 16)
            ins = None
            if cx.on("sync"):
                ins = E.dma_start(out=id_32[:], in_=ident)
            cx.bump("q", ins, 16)
            ins = None
            if cx.on("sync"):
                ins = E.dma_start(out=e_pat_sb[:], in_=e_pat)
            cx.bump("q", ins, 16)
            ev["setup_q"] = cx.c["q"]
            ins = None
            if cx.on("vector"):
                ins = E.memset(xr[:, 0:1024], 0.0)
            cx.bump("v", ins)
            ev["xr0_zero"] = cx.c["v"]
            ins = None
            if cx.on("gpsimd"):
                cx.w("v", ev["xr0_zero"])
                ins = E.dma_start(out=xp_d.rearrange("(a p) d -> p a d", p=128),
                                  in_=xr[:, 0:1024].unsqueeze(1).broadcast_to([128, NCHK, D]))
            cx.bump("sa", ins, 16)
            ev["xp_zero"] = ("sa", cx.c["sa"])

            ins = None
            if cx.on("vector"):
                cx.w("g", ev["setup_g_small"])
                E.memset(ones_row[:], 1.0)
                E.memset(zero_row[:], 0.0)
                E.memset(ones_32[:], 1.0)
                ins = E.memset(eps_col[:], 1e-5)
            cx.bump("v", ins)
            ev["setup_v"] = cx.c["v"]

            ev["xch_free", -4] = 0
            ev["xch_free", -3] = 0
            ev["xch_free", -2] = 0
            ev["xch_free", -1] = 0
            std_c3 = std_c

            def p0_ln_a(ch):
                c4 = ch % 4
                ins = None
                if cx.on("sync"):
                    cx.w("v", ev["xch_free", ch - 4])
                    ins = E.dma_start(out=x_ch[c4], in_=x_halo[ch * 128:(ch + 1) * 128, :])
                qs_ = ["qa", "qb", "qg", "qh"][c4]
                cx.bump(qs_, ins, 16)
                ev["xdma", ch] = (qs_, cx.c[qs_])

            def p0_ln_b(ch):
                p = ch % 2
                ins = None
                if cx.on("vector"):
                    cx.w(ev["xdma", ch][0], ev["xdma", ch][1])
                    if ch >= 2:
                        cx.w("s", ev["sqrt", ch - 2])
                    E.bn_stats(stats6[:, p * 12:p * 12 + 6], x_ch[ch % 4][:, 0:512])
                    ins = E.bn_stats(stats6[:, p * 12 + 6:p * 12 + 12], x_ch[ch % 4][:, 512:1024])
                cx.bump("v", ins)
                ins = None
                if cx.on("vector"):
                    cx.w("v", cx.c["v"])
                    ins = E.bn_aggr(stats2[:, p * 2:(p + 1) * 2], stats6[:, p * 12:p * 12 + 12])
                cx.bump("v", ins)
                ev["stats", ch] = cx.c["v"]

            def p0_ln_c(ch):
                p = ch % 2
                ins = None
                if cx.on("scalar"):
                    cx.w("v", ev["stats", ch])
                    if ch >= 2:
                        cx.w("v", ev["xn", ch - 2])
                    ins = E.activation(out=std_c[:, p:p + 1], in_=stats2[:, p * 2 + 1:p * 2 + 2],
                                       func=AF.Sqrt, bias=eps_col[:], scale=1.0)
                cx.bump("s", ins)
                ev["sqrt", ch] = cx.c["s"]

            def p0_ln_d(ch):
                p = ch % 2
                ins = None
                if cx.on("vector"):
                    cx.w("s", ev["sqrt", ch])
                    if ch >= 3 and ("xnw", ch - 2) in ev:
                        cx.w(*ev["xnw", ch - 2])
                    ins = E.reciprocal(r_c[:, p:p + 1], std_c[:, p:p + 1])
                cx.bump("v", ins)
                ins = None
                if cx.on("vector"):
                    cx.w("v", cx.c["v"])
                    ins = E.tensor_scalar(out=xnbf[p][:, :], in0=x_ch[ch % 4],
                                          scalar1=stats2[:, p * 2:p * 2 + 1],
                                          scalar2=r_c[:, p:p + 1],
                                          op0=ALU.subtract, op1=ALU.mult)
                cx.bump("v", ins)
                ev["xn", ch] = cx.c["v"]
                ev["xch_free", ch] = cx.c["v"]

                if 1 <= ch <= 16:
                    qs2 = "qc" if ch % 2 == 1 else "qd"
                    ins = None
                    if cx.on("sync"):
                        cx.w("v", ev["xn", ch])
                        ins = E.dma_start(out=xn_d[(ch - 1) * 128:ch * 128, :],
                                          in_=xnbf[ch % 2][:, :])
                    cx.bump(qs2, ins, 16)
                    ev["xnw", ch] = (qs2, cx.c[qs2])

            def p0_ln_e(ch):
                p = ch % 2
                ins = None
                if cx.on("tensor"):
                    cx.w("v", ev["xn", ch])
                    cx.w("q", ev["setup_q"])
                    if ch >= 2:
                        cx.w("s", ev["xnT", ch - 2])
                    for hb in range(HB):
                        ins = E.transpose(psT[:, p * 1024 + hb * 128:p * 1024 + (hb + 1) * 128],
                                          xnbf[p][:, hb * 128:(hb + 1) * 128], id_bf[:])
                cx.bump("t", ins)
                ev["xtr", ch] = cx.c["t"]

                ins = None
                if cx.on("scalar"):
                    cx.w("t", ev["xtr", ch])
                    dst = xnT[:].rearrange("p (b c) -> p b c", b=HB)[:, 0:HB, ch * 128:(ch + 1) * 128]
                    src = psT[:, p * 1024:p * 1024 + 1024].rearrange("p (b c) -> p b c", b=HB)
                    ins = E.copy(out=dst, in_=src)
                cx.bump("s", ins)
                ev["xnT", ch] = cx.c["s"]

            def p0_gate(ch):
                greg = psB[:, (ch % 2) * 512:(ch % 2) * 512 + 8]
                if 1 <= ch <= 16:
                    ins = None
                    if cx.on("tensor"):
                        cx.w("s", ev["xnT", ch])
                        if ch >= 3:
                            cx.w("v", ev["mask", ch - 2])
                        for hb in range(HB):
                            ins = E.matmul(greg,
                                           xnT[:, hb * TH + ch * 128: hb * TH + (ch + 1) * 128],
                                           gate_bf[:, hb * NE:(hb + 1) * NE],
                                           start=(hb == 0), stop=(hb == HB - 1))
                    cx.bump("t", ins)
                    ev["gate", ch] = cx.c["t"]
                else:
                    ev["gate", ch] = cx.c["t"]

            def p0_proj(ch):
                p = ch % 2
                kreg = psA[:, p * 512:p * 512 + 128]
                vreg = psA[:, 1024 + p * 512:1024 + p * 512 + 128]
                greg = psB[:, p * 512:p * 512 + 8]
                ins = None
                if cx.on("tensor"):
                    cx.w("s", ev["xnT", ch])
                    if ch >= 2:
                        cx.w("s", ev["kev", ch - 2])
                    for hb in range(HB):
                        E.matmul(kreg, w_k[:, hb * 128:(hb + 1) * 128],
                                 xnT[:, hb * TH + ch * 128: hb * TH + (ch + 1) * 128],
                                 start=(hb == 0), stop=False)
                    ins = E.matmul(kreg, bk_row[:],
                                   edge_bf[0:1, ch * 128:(ch + 1) * 128],
                                   start=False, stop=True)
                cx.bump("t", ins)
                ev["kmm", ch] = cx.c["t"]

                ins = None
                if cx.on("tensor"):
                    if ch >= 2:
                        cx.w("s", ev["vev2", ch - 2])
                    for hb in range(HB):
                        E.matmul(vreg,
                                 xnT[:, hb * TH + ch * 128: hb * TH + (ch + 1) * 128],
                                 w_v[:, hb * 128:(hb + 1) * 128],
                                 start=(hb == 0), stop=False)
                    ins = E.matmul(vreg, edge_bf[0:1, ch * 128:(ch + 1) * 128],
                                   bv_row[:], start=False, stop=True)
                cx.bump("t", ins)
                ev["vmm", ch] = cx.c["t"]

                ins = None
                if cx.on("scalar"):
                    cx.w("t", ev["kmm", ch])
                    E.copy(out=KT0[0:64, ch * 128:(ch + 1) * 128], in_=kreg[0:64, :])
                    ins = E.copy(out=KT1[0:64, ch * 128:(ch + 1) * 128], in_=kreg[64:128, :])
                cx.bump("s", ins)
                ev["kev", ch] = cx.c["s"]

                ins = None
                if cx.on("scalar"):
                    cx.w("t", ev["vmm", ch])
                    ins = E.copy(out=V[:, ch * 128:(ch + 1) * 128], in_=vreg)
                cx.bump("s", ins)
                ev["vev2", ch] = cx.c["s"]

                if 1 <= ch <= 16:
                    t_idx = ch - 1
                    ins = None
                    if cx.on("vector"):
                        cx.w("t", ev["gate", ch])
                        ins = E.reduce_max(std_c[:, 2:3], greg, axis=AX)
                    cx.bump("v", ins)
                    ins = None
                    if cx.on("vector"):
                        cx.w("v", cx.c["v"])
                        ins = E.tensor_scalar(out=masks[:, t_idx * NE:(t_idx + 1) * NE],
                                              in0=greg, scalar1=std_c[:, 2:3],
                                              scalar2=None, op0=ALU.is_ge)
                    cx.bump("v", ins)
                    ev["mask", ch] = cx.c["v"]
                else:
                    ev["mask", ch] = cx.c["v"]

            for ch in range(NCH + 1):
                if ch == 8:
                    ins = None
                    if cx.on("gpsimd"):
                        cx.w("v", ev["xn", 6])
                        ins = E.dma_start(out=w_e[0][:, :].rearrange("p (b d) -> p b d", b=HB),
                                          in_=expert_t[0].rearrange("(b p) d -> p b d", p=128))
                    cx.bump("ge0", ins, 16)
                    ev["we", 0] = ("ge0", cx.c["ge0"])
                    ins = None
                    if cx.on("gpsimd"):
                        ins = E.dma_start(out=w_q[:].rearrange("p (b d) -> p b d", b=HB),
                                          in_=wq.rearrange("(b p) d -> p b d", p=128))
                    cx.bump("g2", ins, 16)
                    ins = None
                    if cx.on("gpsimd"):
                        ins = E.dma_start(out=w_o[:].rearrange("p (b d) -> p b d", b=HB),
                                          in_=wo.rearrange("(b p) d -> p b d", p=128))
                    cx.bump("g2", ins, 16)
                    ev["wqo"] = cx.c["g2"]
                if ch < NCH:
                    p0_ln_a(ch)
                    p0_ln_b(ch)
                    p0_ln_c(ch)
                if 0 <= ch - 2 < NCH:
                    p0_gate(ch - 2)
                if 0 <= ch - 1 < NCH:
                    p0_ln_d(ch - 1)
                    p0_ln_e(ch - 1)
                if 0 <= ch - 2 < NCH:
                    p0_proj(ch - 2)
            p0_proj(NCH - 1)

            ev["p0_t"] = cx.c["t"]
            ev["p0_s"] = cx.c["s"]
            ev["p0_v"] = cx.c["v"]

            NA = NT * 8

            def coords(n):
                return (n // 8, (n // 4) % 2, n % 4)

            def emit_prologue(G):
                for mb in range(HB):
                    qreg = psB[:, (mb % 2) * 512:(mb % 2) * 512 + 512]
                    ins = None
                    if cx.on("tensor"):
                        if G == 0 and mb == 0:
                            cx.w("g2", ev["wqo"])
                            cx.w("s", ev["g", NT - 1])
                            cx.w("s", ev["zev", NCHK - 1])
                        if mb >= 2:
                            cx.w("v", ev["qt_ev", G, mb - 2])
                        elif G >= 1:
                            cx.w("v", ev["qt_ev", G - 1, HB - 2 + mb])
                        if G >= 1 and mb <= 1:
                            cx.w("v", ev["combm", G * 4 - 1])
                        for kb in range(HB):
                            ins = E.matmul(qreg,
                                           w_q[:, kb * D + mb * 128: kb * D + (mb + 1) * 128],
                                           xnT[:, kb * TH + 128 + G * 512: kb * TH + 128 + (G + 1) * 512],
                                           start=(kb == 0), stop=(kb == HB - 1))
                    cx.bump("t", ins)
                    ev["qt_mm", G, mb] = cx.c["t"]

                    ins = None
                    if cx.on("vector"):
                        cx.w("t", ev["qt_mm", G, mb])
                        if G == 0 and mb == 0:
                            cx.w("q", ev["setup_q"])
                        E.tensor_scalar(out=QT[0][G % 2][0:64, mb * 512:(mb + 1) * 512],
                                        in0=qreg[0:64, :],
                                        scalar1=bq_cols[0:64, mb:mb + 1],
                                        scalar2=None, op0=ALU.add)
                        ins = E.tensor_scalar(out=QT[1][G % 2][0:64, mb * 512:(mb + 1) * 512],
                                              in0=qreg[64:128, :],
                                              scalar1=bq_cols[64:128, mb:mb + 1],
                                              scalar2=None, op0=ALU.add)
                    cx.bump("v", ins)
                    ev["qt_ev", G, mb] = cx.c["v"]

            def emit_passA(n):
                t, g, hq = coords(n)
                G, tl = t // 4, t % 4
                h_ = n % 2
                base = h_ * 1024
                heads = [g * 8 + hq * 2 + i for i in range(2)]
                ins = None
                if cx.on("tensor"):
                    if n % 32 < 8:
                        cx.w("v", ev["qt_ev", G, hq * 2 + 1])
                    if n >= 2:
                        p2 = coords(n - 2)
                        cx.w("s", ev[("pAexp",) + p2])
                        cx.w("v", ev[("mx",) + p2])
                    for i in range(2):
                        E.matmul(psA[:, base + i * 512:base + i * 512 + 384],
                                 id_bf[:], mka[:], start=True, stop=False)
                    for i, H in enumerate(heads):
                        j = hq * 2 + i
                        qs = QT[g][G % 2][:,
                                          j * 512 + tl * 128:j * 512 + (tl + 1) * 128]
                        ins = E.matmul(psA[:, base + i * 512:base + i * 512 + 384], qs,
                                       KTg[g][:, t * 128: t * 128 + 384],
                                       start=False, stop=True)
                cx.bump("t", ins)
                ev["pA", t, g, hq] = cx.c["t"]

            def e_mx(n):
                t, g, hq = coords(n)
                p = n % 2
                base = (n % 2) * 1024
                ins = None
                if cx.on("vector"):
                    cx.w("t", ev["pA", t, g, hq])
                    if n >= 2:
                        cx.w("s", ev[("pAexp",) + coords(n - 2)])
                    ps3 = psA[:, base:base + 1024].rearrange(
                        "p (a b) -> p a b", a=2)[:, 0:2, 0:384]
                    ins = E.reduce_max(nmx[:, p * 2:p * 2 + 2], ps3,
                                       axis=AX, negate=True)
                cx.bump("v", ins)
                ev["mx", t, g, hq] = cx.c["v"]

            def e_exp(n):
                t, g, hq = coords(n)
                p = n % 2
                h_ = n % 2
                base = h_ * 1024
                ins = None
                if cx.on("scalar"):
                    cx.w("v", ev["mx", t, g, hq])
                    if n >= 2:
                        cx.w("v", ev[("norm",) + coords(n - 2)])
                    for i in range(2):
                        ins = E.activation(out=E_scr[:, (h_ * 2 + i) * 384:(h_ * 2 + i + 1) * 384],
                                           in_=psA[:, base + i * 512:base + i * 512 + 384],
                                           func=AF.Exp, bias=nmx[:, p * 2 + i:p * 2 + i + 1],
                                           scale=1.0, accum_out=den[:, p * 2 + i:p * 2 + i + 1])
                cx.bump("s", ins)
                ev["pAexp", t, g, hq] = cx.c["s"]

            def e_norm(n):
                t, g, hq = coords(n)
                p = n % 2
                h_ = n % 2
                ins = None
                if cx.on("vector"):
                    cx.w("s", ev["pAexp", t, g, hq])
                    if n >= 2:
                        cx.w("t", ev[("ptr",) + coords(n - 2)])
                    ins = E.reciprocal(rden[:, p * 2:p * 2 + 2], den[:, p * 2:p * 2 + 2])
                cx.bump("v", ins)
                ins = None
                if cx.on("vector"):
                    cx.w("v", cx.c["v"])
                    for i in range(2):
                        ins = E.tensor_scalar(out=E2[:, (p * 2 + i) * 384:(p * 2 + i + 1) * 384],
                                              in0=E_scr[:, (h_ * 2 + i) * 384:(h_ * 2 + i + 1) * 384],
                                              scalar1=rden[:, p * 2 + i:p * 2 + i + 1],
                                              scalar2=None, op0=ALU.mult)
                cx.bump("v", ins)
                ev["norm", t, g, hq] = cx.c["v"]

            def e_tr(n):
                t, g, hq = coords(n)
                p = n % 2
                ins = None
                if cx.on("tensor"):
                    cx.w("v", ev["norm", t, g, hq])
                    if n >= 2:
                        cx.w("v", ev[("etev",) + coords(n - 2)])
                    for i in range(2):
                        for kb in range(3):
                            ins = E.transpose(
                                psT[:, p * 1024 + i * 384 + kb * 128: p * 1024 + i * 384 + (kb + 1) * 128],
                                E2[:, (p * 2 + i) * 384 + kb * 128:(p * 2 + i) * 384 + (kb + 1) * 128],
                                id_bf[:])
                cx.bump("t", ins)
                ev["ptr", t, g, hq] = cx.c["t"]

            def e_etev(n):
                t, g, hq = coords(n)
                p = n % 2
                prev = coords(n - 1) if n >= 1 else None
                ins = None
                if cx.on("vector"):
                    cx.w("t", ev["ptr", t, g, hq])
                    if n == 0:
                        cx.w(*ev["zpw", NCHK - 1])
                    if prev is not None:
                        cx.w("t", ev[("ot",) + prev])
                    ins = E.tensor_copy(out=ET[:, p * 768:(p + 1) * 768],
                                        in_=psT[:, p * 1024:p * 1024 + 768])
                cx.bump("v", ins)
                ev["etev", t, g, hq] = cx.c["v"]

            def e_ot(n):
                t, g, hq = coords(n)
                p = n % 2
                heads = [g * 8 + hq * 2 + i for i in range(2)]
                ins = None
                if cx.on("tensor"):
                    cx.w("v", ev["etev", t, g, hq])
                    if (g, hq) == (0, 0):
                        if t % 4 == 0:
                            cx.w("v", ev["qt_ev", t // 4, HB - 1])
                        if t >= 1:
                            cx.w("v", ev["combmA", t - 1])
                        E.matmul(psB[:, 0:512], ones_row[:], zero_row[:],
                                 start=True, stop=False, skip_group_check=True)
                        if t >= 1:
                            cx.w("v", ev["combm", t - 1])
                        E.matmul(psB[:, 512:1024], ones_row[:], zero_row[:],
                                 start=True, stop=False, skip_group_check=True)
                    for kb in range(3):
                        vs = V[:, (t + kb) * 128 + g * 64:(t + kb) * 128 + g * 64 + 64]
                        for i, H in enumerate(heads):
                            j = hq * 2 + i
                            ins = E.matmul(psB[64 * g:64 * g + 64,
                                               j * 128:j * 128 + 128],
                                           vs,
                                           ET[:, p * 768 + i * 384 + kb * 128: p * 768 + i * 384 + (kb + 1) * 128],
                                           start=False,
                                           stop=(kb == 2 and i == 1 and hq % 2 == 1),
                                           skip_group_check=True)
                cx.bump("t", ins)
                ev["ot", t, g, hq] = cx.c["t"]

            comb_sched = {}

            def emit_comb_mult(t):
                ins = None
                if cx.on("vector"):
                    cx.w("t", ev["sattn", t])
                    ins = E.tensor_tensor(out=sg_scr[t % 2][:, 0:512], in0=psB[:, 0:512],
                                          in1=g_acc[:, t * D:t * D + 512], op=ALU.mult)
                cx.bump("v", ins)
                ev["combmA", t] = cx.c["v"]
                ins = None
                if cx.on("vector"):
                    ins = E.tensor_tensor(out=sg_scr[t % 2][:, 512:1024], in0=psB[:, 512:1024],
                                          in1=g_acc[:, t * D + 512:(t + 1) * D], op=ALU.mult)
                cx.bump("v", ins)
                ev["combm", t] = cx.c["v"]

            def emit_comb_add(t):
                ins = None
                if cx.on("vector"):
                    cx.w("v", ev["combm", t])
                    cx.w(*ev["xdma2", t])
                    if t >= 2:
                        os_sem = "oa" if t % 2 == 0 else "ob"
                        cx.w(os_sem, 16 * (t // 2))
                    ins = E.tensor_tensor(out=out_st[t % 2], in0=sg_scr[t % 2],
                                          in1=x_ch2[t % 2], op=ALU.add)
                cx.bump("v", ins)
                ev["comb", t] = cx.c["v"]
                ins = None
                if cx.on("scalar"):
                    cx.w("v", ev["comb", t])
                    ins = E.dma_start(out=y[t * 128:(t + 1) * 128, :], in_=out_st[t % 2])
                cx.bump("oa" if t % 2 == 0 else "ob", ins, 16)

            def emit_tile_bottom(t, defer=True):
                ins = None
                if cx.on("scalar"):
                    cx.w("t", ev["ot", t, 1, 3])
                    ins = E.copy(out=OT[:, 0:256], in_=psB[:, 0:256])
                cx.bump("s", ins)
                ev["otev0", t] = cx.c["s"]
                ins = None
                if cx.on("scalar"):
                    ins = E.copy(out=OT[:, 256:512], in_=psB[:, 256:512])
                cx.bump("s", ins)
                ev["otev1", t] = cx.c["s"]
                ins = None
                if cx.on("scalar"):
                    ins = E.copy(out=OT[:, 512:1024], in_=psB[:, 512:1024])
                cx.bump("s", ins)
                ev["otev", t] = cx.c["s"]

                ins = None
                if cx.on("tensor"):
                    cx.w("s", ev["otev0", t])
                    cx.w("g2", ev["wqo"])
                    E.matmul(psB[:, 0:512], ones_row[:], bo_row[0:1, 0:512],
                             start=True, stop=False)
                    for dqb in range(2):
                        os_ = OT[:, dqb * 128:(dqb + 1) * 128]
                        E.matmul(psB[:, 0:512], os_, w_o[:, dqb * D: dqb * D + 512],
                                 start=False, stop=False)
                    cx.w("s", ev["otev1", t])
                    for dqb in range(2, 4):
                        os_ = OT[:, dqb * 128:(dqb + 1) * 128]
                        E.matmul(psB[:, 0:512], os_, w_o[:, dqb * D: dqb * D + 512],
                                 start=False, stop=False)
                    cx.w("s", ev["otev", t])
                    E.matmul(psB[:, 512:1024], ones_row[:], bo_row[0:1, 512:1024],
                             start=True, stop=False)
                    for dqb in range(4):
                        os_ = OT[:, dqb * 128:(dqb + 1) * 128]
                        E.matmul(psB[:, 512:1024], os_, w_o[:, dqb * D + 512:(dqb + 1) * D],
                                 start=False, stop=False)
                    for dqb in range(4, HB):
                        os_ = OT[:, dqb * 128:(dqb + 1) * 128]
                        E.matmul(psB[:, 0:512], os_, w_o[:, dqb * D: dqb * D + 512],
                                 start=False, stop=(dqb == HB - 1))
                        ins = E.matmul(psB[:, 512:1024], os_, w_o[:, dqb * D + 512:(dqb + 1) * D],
                                       start=False, stop=(dqb == HB - 1))
                cx.bump("t", ins)
                ev["sattn", t] = cx.c["t"]

                for td in ([0, 1] if t == 0 else ([t + 1] if t < NT - 1 else [])):
                    qs_ = "qa" if td % 2 == 0 else "qb"
                    ins = None
                    if cx.on("sync"):
                        if td >= 2:
                            cx.w("v", ev["comb", td - 2])
                        else:
                            cx.w("v", ev["xch_free", 16 + (td % 2)])
                        ins = E.dma_start(out=x_ch2[td % 2],
                                          in_=x_halo[128 + td * 128: 256 + td * 128, :])
                    cx.bump(qs_, ins, 16)
                    ev["xdma2", td] = (qs_, cx.c[qs_])

                if defer:
                    nb = 8 * t + 10
                    comb_sched[nb + 1] = ("m", t)
                    comb_sched[nb + 2] = ("a", t)
                else:
                    emit_comb_mult(t)
                    emit_comb_add(t)

            ins = None
            if cx.on("tensor"):
                cx.w("s", ev["vev2", NCH - 1])
                cx.w("v", ev["mask", 16])
                cx.w("q", ev["setup_q"])
                for t in range(NT):
                    ins = E.transpose(psA[0:8, t * 128:(t + 1) * 128],
                                      masks[:, t * NE:(t + 1) * NE], id_32[:])
            cx.bump("t", ins)
            ev["pk_mt"] = cx.c["t"]

            ins = None
            if cx.on("scalar"):
                cx.w("t", ev["pk_mt"])
                ins = E.copy(out=maskT, in_=psA[0:8, 0:2048])
            cx.bump("s", ins)
            ev["pk_mtev"] = cx.c["s"]

            ins = None
            if cx.on("vector"):
                cx.w("s", ev["pk_mtev"])
                E.tensor_tensor_scan(out=Rr, data0=maskT, data1=maskT,
                                     initial=0.0, op0=ALU.add, op1=ALU.max)
                E.drain()
                ins = E.tensor_tensor(out=Rr, in0=Rr, in1=maskT, op=ALU.mult)
            cx.bump("v", ins)
            ev["pk_scan"] = cx.c["v"]

            ins = None
            if cx.on("tensor"):
                cx.w("v", ev["pk_scan"])
                for t in range(NT):
                    ins = E.transpose(psB[:, t * 8:(t + 1) * 8],
                                      Rr[0:8, t * 128:(t + 1) * 128], id_32[0:8, 0:8])
            cx.bump("t", ins)
            ev["pk_rt"] = cx.c["t"]

            ins = None
            if cx.on("scalar"):
                cx.w("t", ev["pk_rt"])
                ins = E.copy(out=Rcol, in_=psB[:, 0:128])
            cx.bump("s", ins)
            ev["pk_rcev"] = cx.c["s"]

            ins = None
            if cx.on("vector"):
                cx.w("s", ev["pk_rcev"])
                E.reduce_sum(Rtok, Rcol.rearrange("p (t e) -> p t e", e=NE), axis=AX)
                E.tensor_tensor(out=emsk, in0=masks[:], in1=e_pat_sb[:], op=ALU.mult)
                E.drain()
                E.reduce_sum(etok, emsk.rearrange("p (t e) -> p t e", e=NE), axis=AX)
                E.drain()
                E.tensor_scalar(out=gidx_f, in0=Rtok, scalar1=-1.0, scalar2=float(CAP - 1),
                                op0=ALU.add, op1=ALU.min)
                E.drain()
                E.scalar_tensor_tensor(out=gidx_f, in0=etok, scalar=float(CAP),
                                       in1=gidx_f, op0=ALU.mult, op1=ALU.add)
                E.drain()
                E.tensor_scalar_min(out=gidx_f, in0=gidx_f, scalar1=float(NE * CAP - 1))
                E.drain()
                ins = E.tensor_copy(out=gidx_i[:], in_=gidx_f)
            cx.bump("v", ins)
            ev["pk_gidx"] = cx.c["v"]

            for b2 in range(8):
                buf = xr if b2 % 2 == 0 else xnTp
                qs2 = "qc" if b2 % 2 == 0 else "qd"
                ins = None
                if cx.on("sync"):
                    if b2 == 0:
                        cx.w(*ev["xp_zero"])
                        for chh in range(1, 17):
                            cx.w(*ev["xnw", chh])
                    if b2 >= 2:
                        cx.w(*ev["pk_schb", b2 - 2])
                    ins = E.dma_start(
                        out=buf[:, 0:2048].rearrange("p (a d) -> p a d", a=2),
                        in_=xn_d[b2 * 256:(b2 + 1) * 256, :].rearrange("(a p) d -> p a d", p=128))
                cx.bump(qs2, ins, 16)
                ev["pk_rdhb", b2] = (qs2, cx.c[qs2])

                ss = "sa" if b2 % 2 == 0 else "sb"
                for q in range(2):
                    t = b2 * 2 + q
                    ins = None
                    if cx.on("gpsimd"):
                        if b2 == 0 and q == 0:
                            cx.w("v", ev["pk_gidx"])
                            cx.w(*ev["xp_zero"])
                        if q == 0:
                            cx.w(*ev["pk_rdhb", b2])
                        ins = E.indirect_dma_start(
                            out=xp_d, out_offset=IndirectOffsetOnAxis(gidx_i[:, t:t + 1], 0),
                            in_=buf[:, q * 1024:(q + 1) * 1024], in_offset=None)
                    cx.bump(ss, ins, 16)
                ev["pk_schb", b2] = (ss, cx.c[ss])

            def ffn_front(cc):
                e = cc // 4
                pcc = cc % 2
                e = cc // 4
                pcc = cc % 2
                if cc % 4 == 1 and e + 1 < NE:
                    en = e + 1
                    ins = None
                    if cx.on("gpsimd"):
                        if en == 1:
                            cx.w("t", ev["p0_t"])
                        else:
                            cx.w("t", ev["ffn_z", (en - 2) * 4 + 3])
                        ins = E.dma_start(out=w_e[en % 2][:, :].rearrange("p (b d) -> p b d", b=HB),
                                          in_=expert_t[en].rearrange("(b p) d -> p b d", p=128))
                    ges = "ge0" if en % 2 == 0 else "ge1"
                    cx.bump(ges, ins, 16)
                    ev["we", en] = (ges, cx.c[ges])
                qs2 = "qe" if pcc == 0 else "qf"
                ins = None
                if cx.on("sync"):
                    if cc <= 1:
                        cx.w(*ev["pk_schb", 6])
                        cx.w(*ev["pk_schb", 7])
                    if cc >= 2:
                        cx.w("t", ev["ffn_tr", cc - 2])
                    ins = E.dma_start(out=xr[:, pcc * 1024:(pcc + 1) * 1024],
                                      in_=xp_d[cc * 128:(cc + 1) * 128, :])
                cx.bump(qs2, ins, 16)
                ev["xpr", cc] = (qs2, cx.c[qs2])

                ins = None
                if cx.on("tensor"):
                    cx.w(*ev["xpr", cc])
                    if cc >= 2:
                        cx.w("s", ev["ffn_xev", cc - 2])
                    for hb in range(HB):
                        ins = E.transpose(psT[:, pcc * 1024 + hb * 128:pcc * 1024 + (hb + 1) * 128],
                                          xr[:, pcc * 1024 + hb * 128:pcc * 1024 + (hb + 1) * 128],
                                          id_bf[:])
                cx.bump("t", ins)
                ev["ffn_tr", cc] = cx.c["t"]

                ins = None
                if cx.on("scalar"):
                    cx.w("t", ev["ffn_tr", cc])
                    if cc >= 2:
                        cx.w("t", ev["ffn_z", cc - 2])
                    ins = E.copy(out=xnTp[:, pcc * 1024:(pcc + 1) * 1024],
                                 in_=psT[:, pcc * 1024:(pcc + 1) * 1024])
                cx.bump("s", ins)
                ev["ffn_xev", cc] = cx.c["s"]


            def ffn_back(cc):
                e = cc // 4
                pcc = cc % 2
                zb = psA[:, 0:1024] if pcc == 0 else psA[:, 1024:2048]
                ins = None
                if cx.on("tensor"):
                    cx.w("s", ev["ffn_xev", cc])
                    if cc == 0:
                        cx.w(*ev["we", 0])
                    if e > 0 and cc % 4 == 0:
                        cx.w(*ev["we", e])
                    if cc >= 2:
                        cx.w("s", ev["zev", cc - 2])
                    for hb in range(HB):
                        stat = xnTp[:, pcc * 1024 + hb * 128:pcc * 1024 + (hb + 1) * 128]
                        E.matmul(zb[:, 0:512], stat,
                                 w_e[e % 2][:, hb * D: hb * D + 512],
                                 start=(hb == 0), stop=(hb == HB - 1))
                        ins = E.matmul(zb[:, 512:1024], stat,
                                       w_e[e % 2][:, hb * D + 512: (hb + 1) * D],
                                       start=(hb == 0), stop=(hb == HB - 1))
                cx.bump("t", ins)
                ev["ffn_z", cc] = cx.c["t"]

                ins = None
                if cx.on("scalar"):
                    cx.w("t", ev["ffn_z", cc])
                    if cc >= 2:
                        cx.w(*ev["zpw", cc - 2])
                    ins = E.copy(out=zev[:, pcc * 1024:(pcc + 1) * 1024], in_=zb)
                cx.bump("s", ins)
                ev["zev", cc] = cx.c["s"]

                qs3 = "sa" if pcc == 0 else "sb"
                ins = None
                if cx.on("gpsimd"):
                    cx.w("s", ev["zev", cc])
                    ins = E.dma_start(out=zp_d[cc * 128:(cc + 1) * 128, :],
                                      in_=zev[:, pcc * 1024:(pcc + 1) * 1024])
                cx.bump(qs3, ins, 16)
                ev["zpw", cc] = (qs3, cx.c[qs3])


            for cc in range(NCHK + 1):
                if cc < NCHK:
                    ffn_front(cc)
                if cc >= 1:
                    ffn_back(cc - 1)

            for t in range(NT):
                gq = ["sa", "sb", "sc", "sd"][t % 4]
                ins = None
                if cx.on("gpsimd"):
                    if t <= 3:
                        for cc in range(NCHK):
                            cx.w(*ev["zpw", cc])
                    else:
                        cx.w("v", ev["g_add", t - 4])
                    ins = E.indirect_dma_start(
                        out=g_acc[:, t * D:(t + 1) * D], out_offset=None,
                        in_=zp_d, in_offset=IndirectOffsetOnAxis(gidx_i[:, t:t + 1], 0))
                cx.bump(gq, ins, 16)
                ev["gsel", t] = (gq, cx.c[gq])

                pg = t % 2
                ins = None
                if cx.on("tensor"):
                    if t <= 1:
                        cx.w("s", ev["ffn_xev", NCHK - 1])
                    else:
                        cx.w("v", ev["g_add", t - 2])
                    for hb in range(HB):
                        ins = E.transpose(psT[:, pg * 1024 + hb * 128:pg * 1024 + (hb + 1) * 128],
                                          xnT[:, hb * TH + 128 + t * 128: hb * TH + 256 + t * 128],
                                          id_bf[:])
                cx.bump("t", ins)
                ev["g_tr", t] = cx.c["t"]

                ins = None
                if cx.on("vector"):
                    cx.w("t", ev["g_tr", t])
                    cx.w(*ev["gsel", t])
                    if t >= 2:
                        cx.w("s", ev["g", t - 2])
                    ins = E.tensor_tensor(out=sg_scr[t % 2], in0=psT[:, pg * 1024:pg * 1024 + 1024],
                                          in1=g_acc[:, t * D:(t + 1) * D], op=ALU.add)
                cx.bump("v", ins)
                ev["g_add", t] = cx.c["v"]

                ins = None
                if cx.on("scalar"):
                    cx.w("v", ev["g_add", t])
                    ins = E.activation(out=g_acc[:, t * D:(t + 1) * D],
                                       in_=sg_scr[t % 2], func=AF.Silu)
                cx.bump("s", ins)
                ev["g", t] = cx.c["s"]

            for G in range(4):
                n0 = 32 * G
                emit_prologue(G)
                for n in range(n0, n0 + 32):
                    emit_passA(n)
                    if n - 1 >= n0:
                        e_mx(n - 1)
                        e_exp(n - 1)
                    if n - 2 >= n0:
                        e_norm(n - 2)
                    if n in comb_sched:
                        kind, tc = comb_sched[n]
                        (emit_comb_mult if kind == "m" else emit_comb_add)(tc)
                    if n - 2 >= n0:
                        e_tr(n - 2)
                    if n - 3 >= n0:
                        e_etev(n - 3)
                        e_ot(n - 3)
                        if (n - 3) % 8 == 7:
                            emit_tile_bottom((n - 3) // 8)
                ne = n0 + 32
                e_mx(ne - 1); e_exp(ne - 1)
                e_norm(ne - 2); e_tr(ne - 2)
                e_etev(ne - 3); e_ot(ne - 3)
                e_norm(ne - 1); e_tr(ne - 1)
                e_etev(ne - 2); e_ot(ne - 2)
                e_etev(ne - 1); e_ot(ne - 1)
                emit_tile_bottom(4 * G + 3, defer=False)

            n_dbg = 0
            if debug:
                taps = [("d_xnT", xnT[:]), ("d_KT", KT0[0:64, :]), ("d_V", V[:]),
                        ("d_masks", masks[:]), ("d_g", g_acc[:]),
                        ("d_QT", QT[0][0][0:64, 0:2048]), ("d_OT", OT[:]),
                        ("d_den", den[:]), ("d_nmx", nmx[:]), ("d_ET", ET[:])]
                n_dbg = len(taps)
                if cx.on("gpsimd"):
                    cx.w("t", cx.c["t"])
                    cx.w("v", cx.c["v"])
                    cx.w("s", cx.c["s"])
                    for nm, src_ap in taps:
                        E.dma_start(out=dbg[nm], in_=src_ap).then_inc(sems["g"], 16)
                cx.c["g"] += 16 * n_dbg

            if cx.on("sync"):
                cx.w("oa", 16 * (NT // 2))
                cx.w("ob", 16 * (NT // 2))
                if n_dbg:
                    cx.w("g", ev["setup_g_small"] + 16 * n_dbg)

        with nc.Block() as block:
            @block.sync
            def _(E):
                script("sync", E)

            @block.gpsimd
            def _(E):
                script("gpsimd", E)

            @block.tensor
            def _(E):
                script("tensor", E)

            @block.vector
            def _(E):
                script("vector", E)

            @block.scalar
            def _(E):
                script("scalar", E)

    return nc


def _host_inputs(L0_attn_x, L0_attn_Wq, L0_attn_bq, L0_attn_Wk, L0_attn_bk,
                 L0_attn_Wv, L0_attn_bv, L0_attn_Wo, L0_attn_bo,
                 L0_ffn_gate_w, L0_ffn_expert_w):
    f = np.float32
    x = np.asarray(L0_attn_x, f)
    B, S, _ = x.shape
    perm = np.concatenate([np.concatenate([np.arange(64 * j, 64 * j + 64),
                                           np.arange(512 + 64 * j, 512 + 64 * j + 64)])
                           for j in range(8)])
    shared = {
        "wq": np.ascontiguousarray(np.asarray(L0_attn_Wq, f)[:, perm]),
        "wk": np.ascontiguousarray(np.asarray(L0_attn_Wk, f)),
        "wv": np.ascontiguousarray(np.asarray(L0_attn_Wv, f)),
        "wo": np.ascontiguousarray(np.asarray(L0_attn_Wo, f)[perm, :]),
        "gate_t": np.ascontiguousarray(np.asarray(L0_ffn_gate_w, f).T),
        "expert_t": np.ascontiguousarray(np.asarray(L0_ffn_expert_w, f).transpose(0, 2, 1)),
        "bqc": np.ascontiguousarray(np.asarray(L0_attn_bq, f)[perm].reshape(8, 128).T),
        "bkr": np.ascontiguousarray(np.asarray(L0_attn_bk, f)[None, :]),
        "bvr": np.ascontiguousarray(np.asarray(L0_attn_bv, f)[None, :]),
        "bor": np.ascontiguousarray(np.asarray(L0_attn_bo, f)[None, :]),
        "ident": np.eye(128, dtype=f),
        "ident_bf": np.eye(128, dtype=ml_dtypes.bfloat16),
    }
    i = np.arange(128)
    j = np.arange(128)
    M0 = np.where(j[None, :] >= i[:, None], 0.0, NEG).astype(f)
    M2 = np.where(j[None, :] < i[:, None], 0.0, NEG).astype(f)
    Z = np.zeros((128, 128), f)
    shared["mask_a"] = np.ascontiguousarray(np.concatenate([M0, Z, M2], axis=1))
    shared["mask_b"] = np.ascontiguousarray(
        np.concatenate([M0.T.copy(), Z, M2.T.copy()], axis=1))
    shared["e_pat"] = np.ascontiguousarray(
        np.tile(np.arange(8, dtype=f)[None, :], (128, 16)).reshape(128, 128))

    in_maps = []
    for c in range(8):
        b, q = c // 4, c % 4
        lo = q * TC - 128
        hi = q * TC + TC + 128
        xh = np.zeros((TH, D), f)
        eg = np.zeros((1, TH), f)
        lo2, hi2 = max(lo, 0), min(hi, S)
        xh[lo2 - lo: hi2 - lo] = x[b, lo2:hi2]
        eg[0, lo2 - lo: hi2 - lo] = 1.0
        m = dict(shared)
        m["x_halo"] = xh
        m["edge"] = eg
        in_maps.append(m)
    return in_maps


def kernel(**inputs):
    in_maps = _host_inputs(**inputs)
    nc = build_program()
    res = run_bass_kernel_spmd(nc, in_maps, core_ids=list(range(8)))
    out = np.empty((2, 8192, D), np.float32)
    for c in range(8):
        b, q = c // 4, c % 4
        out[b, q * TC:(q + 1) * TC] = res.results[c]["y"]
    return out
